# version 23
# baseline (speedup 1.0000x reference)
import numpy as np
from contextlib import ExitStack

import concourse.bass as bass
import concourse.mybir as mybir
from concourse.bass_utils import run_bass_kernel_spmd

F32 = mybir.dt.float32
BF16 = mybir.dt.bfloat16
I32 = mybir.dt.int32
AF = mybir.ActivationFunctionType
ALU = mybir.AluOpType
AX = mybir.AxisListType

T = 2048
D = 1024
H = 16
HD = 64
NTC = T // 128
NDC = D // 128
DH = 4096
NJC = DH // 128
EPS_LN = 1e-5

PHASES = ("A", "B", "C", "D")
FFN_CUT = None

ENGINES = ("pe", "dve", "act", "pool", "sp", "gp")
STREAM_OF = {"pe": "pe", "dve": "dve", "act": "act", "pool": "gpq", "gp": "gpq", "sp": "sp"}
NO_SELF_EDGE = {"pe", "sp", "gp"}


class SB:

    NLANE = 8

    def __init__(self, nc):
        self.nc = nc
        self.streams = {"pe": [], "dve": [], "act": [], "gpq": [], "sp": []}
        self.count = {e: 0 for e in ("pe", "dve", "act", "pool")}
        self.qcount = {"sp": 0, "gp": 0}
        self.last_w = {}
        self.readers = {}
        self.waited = {}
        self.sems = {}
        self.label = ""

    def _waited(self, eng):
        return self.waited.setdefault(eng, {})

    def op(self, eng, fn, reads=(), writes=(), dma=False):
        need = {}
        for k in reads:
            w = self.last_w.get(k)
            if w is not None:
                need[w[0]] = max(need.get(w[0], 0), w[1])
        for k in writes:
            w = self.last_w.get(k)
            if w is not None:
                need[w[0]] = max(need.get(w[0], 0), w[1])
            for f, c in self.readers.get(k, {}).items():
                need[f] = max(need.get(f, 0), c)
        if dma:
            q = eng
            slot = self.qcount[q] % self.NLANE
            cycle = self.qcount[q] // self.NLANE
            self.qcount[q] += 1
            pid = (q, slot)
            if cycle > 0:
                need[pid] = max(need.get(pid, 0), 16 * cycle)
            my_id, my_cnt = pid, 16 * (cycle + 1)
            wm = self._waited(q)
        else:
            self.count[eng] += 1
            my_id, my_cnt = eng, self.count[eng]
            wm = self._waited(eng)
        waits = []
        for f, c in sorted(need.items(), key=str):
            if f == eng and eng in NO_SELF_EDGE:
                continue
            if c > wm.get(f, 0):
                waits.append((f, c))
                wm[f] = c
        sems = self.sems
        step = 16 if dma else 1
        stream = STREAM_OF[eng]

        lbl = self.label

        def emit(engobj, waits=waits, fn=fn, my_id=my_id, step=step, lbl=lbl):
            for f, c in waits:
                engobj.wait_ge(sems[f], c)
            ins = fn(engobj)
            if lbl:
                ins.annotate(lbl)
            ins.then_inc(sems[my_id], step)

        self.streams[stream].append(emit)
        for k in writes:
            self.last_w[k] = (my_id, my_cnt)
            self.readers[k] = {}
        for k in reads:
            self.readers.setdefault(k, {})[my_id] = my_cnt
        return my_cnt

    def _all_producers(self):
        cur = {e: c for e, c in self.count.items() if c > 0}
        for q in ("sp", "gp"):
            n = self.qcount[q]
            for slot in range(self.NLANE):
                uses = (n - slot + self.NLANE - 1) // self.NLANE
                if uses > 0:
                    cur[(q, slot)] = 16 * uses
        return cur

    def barrier(self):
        cur = self._all_producers()
        sems = self.sems
        for e in ("pe", "dve", "act", "pool", "sp", "gp"):
            wm = self._waited(e)
            waits = []
            for f, c in cur.items():
                if f == e:
                    continue
                if c > wm.get(f, 0):
                    waits.append((f, c))
                    wm[f] = c
            if not waits:
                continue

            def emit(engobj, waits=waits):
                for f, c in waits:
                    engobj.wait_ge(sems[f], c)

            self.streams[STREAM_OF[e]].append(emit)

    def finish(self):
        cur = self._all_producers()
        sems = self.sems

        def emit(engobj):
            for f, c in cur.items():
                engobj.wait_ge(sems[f], c)

        self.streams["sp"].append(emit)


def build_program():
    nc = bass.Bass("TRN2", target_bir_lowering=False, debug=False)
    sb = SB(nc)

    dr = {}
    def din(name, shape, dt=F32):
        dr[name] = nc.dram_tensor(name, list(shape), dt, kind="ExternalInput").ap()

    din("x", (T, D))
    din("W_attn", (D, 3 * D))
    din("b_attn", (3 * D,))
    din("W_proj", (D, D))
    din("b_proj", (D,))
    din("qkv_wT", (D, 3 * D))
    din("qkv_b", (3 * D,))
    din("out_wT", (D, D))
    din("out_b", (D,))
    din("w1T", (D, DH))
    din("convw_t", (128, NJC * 3))
    din("convb_t", (128, NJC))
    din("w2T", (DH, D))
    din("cext_sb", (128, NTC * 64), BF16)
    din("sext_sb", (128, NTC * 64), BF16)
    din("tri", (128, 128))
    din("eye_bf", (128, 128), BF16)
    din("eye_f32", (128, 128))

    out = nc.dram_tensor("out", [T, D], F32, kind="ExternalOutput").ap()

    scr = {}
    def dscr(name, shape, dt):
        scr[name] = nc.dram_tensor(name, list(shape), dt, kind="Internal").ap()

    dscr("qspill", (NTC, NDC, 128, 128), BF16)
    dscr("kspill", (NTC, NDC, 128, 128), BF16)
    dscr("vspill", (NTC, 128, H * 65), BF16)
    dscr("w1qT_d", (NJC, 128, D), BF16)
    dscr("w2qT_d", (2, NJC // 2, 128, 2, 512), BF16)

    with ExitStack() as st:
        for e in ("pe", "dve", "act", "pool"):
            sb.sems[e] = st.enter_context(nc.semaphore(f"sem_{e}"))
        for q in ("sp", "gp"):
            for slot in range(SB.NLANE):
                sb.sems[(q, slot)] = st.enter_context(nc.semaphore(f"sem_{q}{slot}"))

        squad = st.enter_context(nc.psum_tensor("squad", [128, 2048], F32))
        pbank = [st.enter_context(nc.psum_tensor(f"pb{i}", [128, 512], F32)) for i in range(4)]

        x_tm = [st.enter_context(nc.sbuf_tensor(f"x{c}", [128, D], F32)) for c in range(NTC)]

        ctx = dict(nc=nc, sb=sb, dr=dr, scr=scr, out=out, x_tm=x_tm,
                   squad=squad, pbank=pbank, st=st)
        emit_all(ctx)
        sb.finish()

        block = st.enter_context(nc.Block())

        @block.sync
        def _(sync):
            for f in sb.streams["sp"]:
                f(sync)

        @block.gpsimd
        def _(g):
            for f in sb.streams["gpq"]:
                f(g)

        @block.vector
        def _(v):
            for f in sb.streams["dve"]:
                f(v)

        @block.scalar
        def _(s):
            for f in sb.streams["act"]:
                f(s)

        @block.tensor
        def _(t):
            for f in sb.streams["pe"]:
                f(t)

    return nc



def emit_all(ctx):
    nc, sb, dr, st = ctx["nc"], ctx["sb"], ctx["dr"], ctx["st"]
    x_tm, out = ctx["x_tm"], ctx["out"]

    def sbuf(name, shape, dt, stack=st):
        return stack.enter_context(nc.sbuf_tensor(name, list(shape), dt))
    ctx["sbuf"] = sbuf

    C = {}
    ctx["C"] = C
    C["tri"] = sbuf("tri_s", (128, 128), F32)
    C["eyeb"] = sbuf("eyeb_s", (128, 128), BF16)
    C["eyef"] = sbuf("eyef_s", (128, 128), F32)
    C["ones_row"] = sbuf("ones_row", (128, 128), F32)
    C["Cext"] = sbuf("Cext", (128, NTC * 64), BF16)
    C["Sext"] = sbuf("Sext", (128, NTC * 64), BF16)
    C["mv"] = sbuf("mv_all", (128, 2 * NTC), F32)
    C["stat6"] = sbuf("stat6", (128, 12), F32)
    C["rstd"] = sbuf("rstd_all", (128, NTC), F32)
    C["nmr"] = sbuf("nmr_all", (128, NTC), F32)
    C["lnv"] = sbuf("lnv_all", (128, NTC), F32)
    C["wsc"] = sbuf("wsc_all", (128, 4), F32)

    for nm, src in (("tri", "tri"), ("eyeb", "eye_bf"), ("eyef", "eye_f32"),
                    ("Cext", "cext_sb"), ("Sext", "sext_sb")):
        sb.op("sp", lambda e, nm=nm, src=src: e.dma_start(out=C[nm][:], in_=dr[src]),
              writes=[(nm,)], dma=True)
    sb.op("dve", lambda e: e.memset(C["ones_row"][:], 1.0), writes=[("ones_row",)])
    C["k"] = sbuf("kconst", (128, 8), F32)
    for col, val in ((0, EPS_LN), (1, -0.5), (2, 0.125), (3, -1.0), (4, 127.0), (5, 1e-12)):
        sb.op("dve", lambda e, col=col, val=val: e.memset(C["k"][:, col:col + 1], val),
              writes=[("k",)])

    for c in range(NTC):
        sb.op("sp", lambda e, c=c: e.dma_start(
            out=x_tm[c][:], in_=dr["x"][c * 128:(c + 1) * 128, :]),
            writes=[("x", c)], dma=True)

    if "A" in PHASES:
        wq_stack = ExitStack()
        if "B" in PHASES or "D" in PHASES:
            bg = wquant_gen(ctx, wq_stack)
            next(bg, None)
        else:
            bg = None
        emit_attention(ctx, which="a", bg=bg)
        wq_stack.close()
    if "B" in PHASES:
        sb.barrier()
        emit_ffn(ctx, first=True)
    if "C" in PHASES:
        sb.barrier()
        emit_attention(ctx, which="b")
    if "D" in PHASES:
        sb.barrier()
        emit_ffn(ctx, first=False)

    sb.barrier()
    for c in range(NTC):
        sb.op("sp", lambda e, c=c: e.dma_start(
            out=out[c * 128:(c + 1) * 128, :], in_=x_tm[c][:]),
            reads=[("x", c)], dma=True)


def emit_layernorm(sb, C, x_tm):
    for c in range(NTC):
        sb.op("dve", lambda e, c=c: e.bn_stats(out=C["stat6"][:, 0:6], in_=x_tm[c][:, 0:512]),
              reads=[("x", c)], writes=[("stat6",)])
        sb.op("dve", lambda e, c=c: e.bn_stats(out=C["stat6"][:, 6:12], in_=x_tm[c][:, 512:1024]),
              reads=[("x", c)], writes=[("stat6",)])
        sb.op("dve", lambda e, c=c: e.bn_aggr(out=C["mv"][:, 2 * c:2 * c + 2], in_=C["stat6"][:]),
              reads=[("stat6",)], writes=[("mv",)])
    mvv = C["mv"][:].rearrange("p (c two) -> p c two", two=2)
    sb.op("act", lambda e: e.activation(out=C["lnv"][:], in_=mvv[:, :, 1], func=AF.Ln, bias=C["k"][:, 0:1]),
          reads=[("mv",)], writes=[("lnv",)])
    sb.op("act", lambda e: e.activation(out=C["rstd"][:], in_=C["lnv"][:], func=AF.Exp, scale=C["k"][:, 1:2]),
          reads=[("lnv",), ("k",)], writes=[("rstd",)])
    sb.op("dve", lambda e: e.scalar_tensor_tensor(
        out=C["nmr"][:], in0=mvv[:, :, 0], scalar=-1.0, in1=C["rstd"][:],
        op0=ALU.mult, op1=ALU.mult),
        reads=[("mv",), ("rstd",)], writes=[("nmr",)])


def emit_attention(ctx, which, bg=None):
    nc, sb, dr, scr, C = ctx["nc"], ctx["sb"], ctx["dr"], ctx["scr"], ctx["C"]
    x_tm, squad, pbank, sbuf = ctx["x_tm"], ctx["squad"], ctx["pbank"], ctx["sbuf"]
    causal = (which == "a")
    wname = "W_attn" if causal else "qkv_wT"
    bname = "b_attn" if causal else "qkv_b"
    pname = "W_proj" if causal else "out_wT"
    pbias = "b_proj" if causal else "out_b"

    s1 = ExitStack()
    vstage = [sbuf(f"vstage{which}_{i}", (128, H * 65), BF16, stack=s1) for i in range(2)]
    Wqkv = [sbuf(f"Wqkv{which}_{dc}", (128, 3 * D), BF16, stack=s1) for dc in range(NDC)]
    battrow = sbuf(f"battrow_{which}", (1, 3 * D), BF16, stack=s1)
    onesbf = sbuf(f"onesbf_{which}", (1, 128), BF16, stack=s1)
    ln = [sbuf(f"ln{which}_{i}", (128, D), BF16, stack=s1) for i in range(2)]
    lnT = [sbuf(f"lnT{which}_{i}", (128, D), BF16, stack=s1) for i in range(2)]
    qkbf = [sbuf(f"qkbf{which}_{i}", (128, 2048), BF16, stack=s1) for i in range(2)]
    ropet = [sbuf(f"ropet{which}_{i}", (128, 2048), BF16, stack=s1) for i in range(2)]
    ropeq = [sbuf(f"ropeq{which}_{i}", (128, 2048), BF16, stack=s1) for i in range(2)]
    qstage = sbuf(f"qstage_{which}", (128, 1024), BF16, stack=s1)
    kstage = sbuf(f"kstage_{which}", (128, 1024), BF16, stack=s1)
    if not causal:
        s1tmp = sbuf(f"s1tmp_{which}", (128, 1024), F32, stack=s1)
        s1b = sbuf(f"s1b_{which}", (128, 128), F32, stack=s1)

    for dc in range(NDC):
        sb.op("gp", lambda e, dc=dc: e.dma_start(
            out=Wqkv[dc][:], in_=dr[wname][dc * 128:(dc + 1) * 128, :]),
            writes=[("Wqkv", dc)], dma=True)
    sb.op("gp", lambda e: e.dma_start(out=battrow[:], in_=dr[bname][None, :]),
          writes=[("battrow",)], dma=True)
    sb.op("dve", lambda e: e.memset(onesbf[:], 1.0), writes=[("onesbf",)])

    for i in range(2):
        sb.op("dve", lambda e, i=i: e.memset(
            vstage[i][:].rearrange("p (h c65) -> p h c65", c65=65)[:, :, 64:65], 1.0),
            writes=[("vstage", i)])

    emit_layernorm(sb, C, x_tm)

    pending = [None]

    for c in range(NTC):
        sb.label = f"s1{which}_c{c}"
        i = c % 2
        sb.op("act", lambda e, c=c, i=i: e.activation(
            out=ln[i][:], in_=x_tm[c][:], func=AF.Identity,
            scale=C["rstd"][:, c:c + 1], bias=C["nmr"][:, c:c + 1]),
            reads=[("x", c), ("rstd",), ("nmr",)], writes=[("ln", i)])
        tps = squad[:, 0:512].bitcast(BF16)
        for dc in range(NDC):
            sb.op("pe", lambda e, i=i, dc=dc, tps=tps: e.transpose(
                out=tps[:, dc * 128:(dc + 1) * 128],
                in_=ln[i][:, dc * 128:(dc + 1) * 128], identity=C["eyeb"][:]),
                reads=[("ln", i), ("eyeb",)], writes=[("sq", 0)])
        sb.op("dve", lambda e, i=i, tps=tps: e.tensor_copy(out=lnT[i][:], in_=tps),
              reads=[("sq", 0)], writes=[("lnT", i)])
        for g in range(6):
            pb = pbank[g % 2]
            for dc in range(NDC):
                sb.op("pe", lambda e, i=i, dc=dc, g=g, pb=pb: e.matmul(
                    pb[:], lnT[i][:, dc * 128:(dc + 1) * 128],
                    Wqkv[dc][:, g * 512:(g + 1) * 512],
                    start=(dc == 0), stop=False),
                    reads=[("lnT", i), ("Wqkv", dc)], writes=[("pb", g % 2)])
            sb.op("pe", lambda e, g=g, pb=pb: e.matmul(
                pb[:], onesbf[0:1, :], battrow[0:1, g * 512:(g + 1) * 512],
                start=False, stop=True),
                reads=[("onesbf",), ("battrow",)], writes=[("pb", g % 2)])
            if g < 4:
                if causal:
                    dst = qkbf[i][:, g * 512:(g + 1) * 512]
                    sb.op("act", lambda e, dst=dst, pb=pb: e.activation(
                        out=dst, in_=pb[:], func=AF.Identity),
                        reads=[("pb", g % 2)], writes=[("qkbf", i)])
                else:
                    dst = qkbf[i][:, g * 512:(g + 1) * 512].rearrange(
                        "p (h b a) -> p h a b", b=2, a=32)
                    src0 = pb[:].rearrange("p (h a b) -> p h a b", a=32, b=2)
                    sb.op("act", lambda e, dst=dst, src0=src0: e.activation(
                        out=dst, in_=src0, func=AF.Identity),
                        reads=[("pb", g % 2)], writes=[("qkbf", i)])
            else:
                h0 = (g - 4) * 8
                vv = vstage[i][:].rearrange("p (h c65) -> p h c65", c65=65)[:, h0:h0 + 8, 0:64]
                sb.op("act", lambda e, pb=pb, vv=vv: e.activation(
                    out=vv, in_=pb[:].rearrange("p (h q) -> p h q", q=64), func=AF.Identity),
                    reads=[("pb", g % 2)], writes=[("vstage", i)])
        if pending[0] is not None:
            pending[0]()
        qv = qkbf[i][:].rearrange("p (g2 h i2) -> p g2 h i2", h=2, i2=32)
        cs = C["Cext"][:, c * 64:(c + 1) * 64].rearrange("p (h i2) -> p h i2", h=2)[:, None, :, :].broadcast_to((128, 32, 2, 32))
        sx = C["Sext"][:, c * 64:(c + 1) * 64].rearrange("p (h i2) -> p h i2", h=2)[:, None, :, :].broadcast_to((128, 32, 2, 32))
        rt = ropet[i][:].rearrange("p (g2 h i2) -> p g2 h i2", h=2, i2=32)
        rq = ropeq[i][:].rearrange("p (g2 h i2) -> p g2 h i2", h=2, i2=32)
        sb.op("dve", lambda e, qv=qv, sx=sx, rt=rt, i=i: e.tensor_tensor(
            out=rt[:, :, 0, :], in0=qv[:, :, 1, :], in1=sx[:, :, 0, :], op=ALU.mult),
            reads=[("qkbf", i), ("Sext",)], writes=[("ropet", i)])
        sb.op("dve", lambda e, qv=qv, sx=sx, rt=rt, i=i: e.tensor_tensor(
            out=rt[:, :, 1, :], in0=qv[:, :, 0, :], in1=sx[:, :, 1, :], op=ALU.mult),
            reads=[("qkbf", i), ("Sext",)], writes=[("ropet", i)])
        sb.op("dve", lambda e, qv=qv, cs=cs, rq=rq, i=i: e.tensor_tensor(
            out=rq, in0=qv, in1=cs, op=ALU.mult),
            reads=[("qkbf", i), ("Cext",)], writes=[("ropeq", i)])
        sb.op("dve", lambda e, i=i: e.tensor_tensor(
            out=ropeq[i][:], in0=ropeq[i][:], in1=ropet[i][:], op=ALU.add),
            reads=[("ropet", i), ("ropeq", i)], writes=[("ropeq", i)])
        if not causal:
            rv = ropeq[i][:].rearrange("p (g2 q) -> p g2 q", q=64)
            for half in range(2):
                sb.op("pool", lambda e, rv=rv, half=half: e.tensor_tensor(
                    out=s1tmp[:].rearrange("p (g2 q) -> p g2 q", q=32),
                    in0=rv[:, :, half * 32:(half + 1) * 32],
                    in1=rv[:, :, half * 32:(half + 1) * 32], op=ALU.mult),
                    reads=[("ropeq", i)], writes=[("s1tmp",)])
                sb.op("dve", lambda e, half=half: e.tensor_reduce(
                    out=s1b[:, half * 32:(half + 1) * 32],
                    in_=s1tmp[:].rearrange("p (g2 q) -> p g2 q", q=32),
                    axis=AX.X, op=ALU.add),
                    reads=[("s1tmp",)], writes=[("s1b",)])
            sb.op("dve", lambda e: e.tensor_tensor(
                out=s1b[:, 64:96], in0=s1b[:, 0:32], in1=s1b[:, 32:64], op=ALU.add),
                reads=[("s1b",)], writes=[("s1b",)])
            sb.op("act", lambda e: e.activation(out=s1b[:, 96:128], in_=s1b[:, 64:96], func=AF.Ln),
                  reads=[("s1b",)], writes=[("s1b",)])
            sb.op("act", lambda e: e.activation(out=s1b[:, 64:96], in_=s1b[:, 96:128], func=AF.Exp, scale=C["k"][:, 1:2]),
                  reads=[("s1b",), ("k",)], writes=[("s1b",)])
            sb.op("dve", lambda e: e.tensor_scalar(
                out=s1b[:, 64:96], in0=s1b[:, 64:96], scalar1=1e12, scalar2=None, op0=ALU.min),
                reads=[("s1b",)], writes=[("s1b",)])
            sb.op("dve", lambda e, rv=rv: e.tensor_tensor(
                out=rv, in0=rv, in1=s1b[:, 64:96][:, :, None].broadcast_to((128, 32, 64)),
                op=ALU.mult),
                reads=[("ropeq", i), ("s1b",)], writes=[("ropeq", i)])

        def mk_pending(c=c, i=i):
            def emit_tail():
                tq = squad[:, 512:1024].bitcast(BF16)
                tk = squad[:, 1024:1536].bitcast(BF16)
                for dc in range(NDC):
                    sb.op("pe", lambda e, i=i, dc=dc, tq=tq: e.transpose(
                        out=tq[:, dc * 128:(dc + 1) * 128],
                        in_=ropeq[i][:, dc * 128:(dc + 1) * 128], identity=C["eyeb"][:]),
                        reads=[("ropeq", i), ("eyeb",)], writes=[("sq", 1)])
                    sb.op("pe", lambda e, i=i, dc=dc, tk=tk: e.transpose(
                        out=tk[:, dc * 128:(dc + 1) * 128],
                        in_=ropeq[i][:, 1024 + dc * 128:1024 + (dc + 1) * 128], identity=C["eyeb"][:]),
                        reads=[("ropeq", i), ("eyeb",)], writes=[("sq", 2)])
                sb.op("act", lambda e, tq=tq: e.activation(out=qstage[:], in_=tq, func=AF.Identity),
                      reads=[("sq", 1)], writes=[("qstage",)])
                sb.op("act", lambda e, tk=tk: e.activation(out=kstage[:], in_=tk, func=AF.Identity),
                      reads=[("sq", 2)], writes=[("kstage",)])
                sb.op("sp", lambda e, c=c, i=i: e.dma_start(
                    out=scr["vspill"][c], in_=vstage[i][:]),
                    reads=[("vstage", i)], writes=[("vspill", c)], dma=True)
                sb.op("sp", lambda e, c=c: e.dma_start(
                    out=scr["qspill"][c].rearrange("d p q -> p d q"),
                    in_=qstage[:].rearrange("p (d q) -> p d q", d=8)),
                    reads=[("qstage",)], writes=[("qspill", c)], dma=True)
                sb.op("sp", lambda e, c=c: e.dma_start(
                    out=scr["kspill"][c].rearrange("d p q -> p d q"),
                    in_=kstage[:].rearrange("p (d q) -> p d q", d=8)),
                    reads=[("kstage",)], writes=[("kspill", c)], dma=True)
            return emit_tail

        pending[0] = mk_pending()
        if bg is not None:
            for _ in range(9):
                next(bg, None)
    pending[0]()

    s1.close()
    sb.barrier()

    a2 = ExitStack()
    ofm = [sbuf(f"ofm{which}_{p}", (128, 2048), BF16, stack=a2) for p in range(NDC)]
    vbuf = [sbuf(f"vbuf{which}_{i}", (128, NTC * 130), BF16, stack=a2) for i in range(2)]
    Wp = [sbuf(f"Wp{which}_{dc}", (128, D), BF16, stack=a2) for dc in range(NDC)]
    bpro = sbuf(f"bpro_{which}", (128, D), F32, stack=a2)
    qbuf = [sbuf(f"qbuf{which}_{i}", (128, 2048), BF16, stack=a2) for i in range(2)]
    kbuf = [sbuf(f"kbuf{which}_{i}", (128, 2048), BF16, stack=a2) for i in range(2)]
    expq = [sbuf(f"expq{which}_{i}", (128, 1024), BF16, stack=a2) for i in range(2)]
    oh2 = [sbuf(f"oh2_{which}{j}", (64, 512), BF16, stack=a2) for j in range(2)]
    recip = sbuf(f"recip_{which}", (65, 1024), F32, stack=a2)
    lnden = sbuf(f"lnden_{which}", (65, 1024), F32, stack=a2)
    otb = [sbuf(f"otb_{which}{j}", (65, 1024), F32, stack=a2) for j in range(2)]
    prtmp = [sbuf(f"prtmp{which}_{i}", (128, 512), F32, stack=a2) for i in range(2)]

    for dc in range(NDC):
        sb.op("gp", lambda e, dc=dc: e.dma_start(
            out=Wp[dc][:], in_=dr[pname][dc * 128:(dc + 1) * 128, :]),
            writes=[("Wp", dc)], dma=True)
    sb.op("sp", lambda e: e.dma_start(out=bpro[:], in_=dr[pbias][None, :].partition_broadcast(128)),
          writes=[("bpro",)], dma=True)

    def fetch_pair(p):
        i = p % 2
        sb.op("sp", lambda e, p=p, i=i: e.dma_start(
            out=qbuf[i][:].rearrange("p (t q) -> p t q", q=128),
            in_=scr["qspill"][:, p].rearrange("t p q -> p t q")),
            reads=[("qspill", c2) for c2 in range(NTC)], writes=[("qbuf", i)], dma=True)
        sb.op("sp", lambda e, p=p, i=i: e.dma_start(
            out=kbuf[i][:].rearrange("p (t q) -> p t q", q=128),
            in_=scr["kspill"][:, p].rearrange("t p q -> p t q")),
            reads=[("kspill", c2) for c2 in range(NTC)], writes=[("kbuf", i)], dma=True)
        sb.op("sp", lambda e, p=p, i=i: e.dma_start(
            out=vbuf[i][:].rearrange("p (t v) -> p t v", v=130),
            in_=scr["vspill"][:, :, 2 * p * 65:(2 * p + 2) * 65].rearrange("t p v -> p t v")),
            reads=[("vspill", c2) for c2 in range(NTC)], writes=[("vbuf", i)], dma=True)

    segs = []
    for p in range(NDC):
        for g in range(4):
            nch = 4 * g + 4 if causal else NTC
            segs.append((p, g, nch))
    units = []
    for si, (p, g, nch) in enumerate(segs):
        for cch in range(nch):
            units.append((si, cch))
    nunits = len(units)

    def emit_S(u):
        si, cch = units[u]
        p, g, nch = segs[si]
        i = p % 2
        qt0 = g * 512
        sqh = u % 2
        for hi in range(2):
            sb.op("pe", lambda e, i=i, cch=cch, hi=hi, sqh=sqh, qt0=qt0: e.matmul(
                squad[:, sqh * 1024 + hi * 512:sqh * 1024 + (hi + 1) * 512],
                kbuf[i][hi * 64:(hi + 1) * 64, cch * 128:(cch + 1) * 128],
                qbuf[i][hi * 64:(hi + 1) * 64, qt0:qt0 + 512],
                start=True, stop=True),
                reads=[("kbuf", i), ("qbuf", i)], writes=[("sq", 2 * sqh + hi)])

    def emit_exp(u):
        si, cch = units[u]
        p, g, nch = segs[si]
        qt0 = g * 512
        sqh = u % 2
        eb = u % 2
        sb.op("act", lambda e, sqh=sqh, eb=eb: e.activation(
            out=expq[eb][:, 0:1024],
            in_=squad[:, sqh * 1024:(sqh + 1) * 1024], func=AF.Exp, scale=C["k"][:, 2:3]),
            reads=[("sq", 2 * sqh), ("sq", 2 * sqh + 1)], writes=[("expq", eb)])
        if causal:
            lo = cch * 128 - qt0
            if 0 <= lo < 512:
                for hi in range(2):
                    if lo > 0:
                        sb.op("dve", lambda e, eb=eb, hi=hi, lo=lo: e.memset(
                            expq[eb][:, hi * 512:hi * 512 + lo], 0.0),
                            writes=[("expq", eb)])
                    sb.op("dve", lambda e, eb=eb, hi=hi, lo=lo: e.tensor_tensor(
                        out=expq[eb][:, hi * 512 + lo:hi * 512 + lo + 128],
                        in0=expq[eb][:, hi * 512 + lo:hi * 512 + lo + 128],
                        in1=C["tri"][:], op=ALU.mult),
                        reads=[("tri",)], writes=[("expq", eb)])

    def emit_AV(u):
        si, cch = units[u]
        p, g, nch = segs[si]
        i = p % 2
        ob = 2 * (si % 2)
        eb = u % 2
        first = (cch == 0)
        last = (cch == nch - 1)
        for hi in range(2):
            v0 = cch * 130 + hi * 65
            sb.op("pe", lambda e, i=i, hi=hi, v0=v0, eb=eb, ob=ob, first=first, last=last: e.matmul(
                pbank[ob + hi][0:65, :],
                vbuf[i][:, v0:v0 + 65],
                expq[eb][:, hi * 512:(hi + 1) * 512],
                start=first, stop=last),
                reads=[("expq", eb), ("vbuf", i)], writes=[("pb", ob + hi)])

    def emit_finalize(si):
        p, g, nch = segs[si]
        ob = 2 * (si % 2)
        par = si % 2
        sb.op("dve", lambda e, ob=ob, par=par: e.tensor_copy(
            out=otb[par][0:65, 0:512], in_=pbank[ob][0:65, :]),
            reads=[("pb", ob)], writes=[("otb", par)])
        sb.op("dve", lambda e, ob=ob, par=par: e.tensor_copy(
            out=otb[par][0:65, 512:1024], in_=pbank[ob + 1][0:65, :]),
            reads=[("pb", ob + 1)], writes=[("otb", par)])
        sb.op("act", lambda e, par=par: e.activation(
            out=lnden[64:65, 0:1024], in_=otb[par][64:65, 0:1024], func=AF.Ln),
            reads=[("otb", par)], writes=[("lnden",)])
        sb.op("act", lambda e: e.activation(
            out=recip[64:65, 0:1024], in_=lnden[64:65, 0:1024],
            func=AF.Exp, scale=C["k"][64:65, 3:4]),
            reads=[("lnden",), ("k",)], writes=[("recip",)])
        for hi in range(2):
            sb.op("pe", lambda e, hi=hi, ob=ob: e.matmul(
                pbank[ob + hi][0:64, :], C["ones_row"][64:65, 0:64],
                recip[64:65, hi * 512:(hi + 1) * 512],
                start=True, stop=True),
                reads=[("ones_row",), ("recip",)], writes=[("pb", ob + hi)])
        sb.op("dve", lambda e, p=p, g=g, ob=ob, par=par: e.tensor_tensor(
            out=ofm[p][0:64, g * 512:(g + 1) * 512], in0=otb[par][0:64, 0:512],
            in1=pbank[ob][0:64, :], op=ALU.mult),
            reads=[("otb", par), ("pb", ob)], writes=[("ofm", p)])
        sb.op("dve", lambda e, ob=ob, par=par: e.tensor_tensor(
            out=oh2[par][0:64, :], in0=otb[par][0:64, 512:1024], in1=pbank[ob + 1][0:64, :], op=ALU.mult),
            reads=[("otb", par), ("pb", ob + 1)], writes=[("oh2", par)])
        sb.op("sp", lambda e, p=p, g=g, par=par: e.dma_start(
            out=ofm[p][64:128, g * 512:(g + 1) * 512], in_=oh2[par][0:64, :]),
            reads=[("oh2", par)], writes=[("ofm", p)], dma=True)

    fetch_pair(0)
    fetch_pair(1)
    LOOK = 2
    for u in range(LOOK):
        if u < nunits:
            emit_S(u)
    if nunits > 0:
        emit_exp(0)
    for u in range(nunits):
        si, cch = units[u]
        p, g, nch = segs[si]
        if g == 0 and cch == 0 and 1 <= p < NDC - 1:
            fetch_pair(p + 1)
        if u + LOOK < nunits:
            emit_S(u + LOOK)
        if u + 1 < nunits:
            emit_exp(u + 1)
        emit_AV(u)
        if cch == nch - 1:
            emit_finalize(si)
        if bg is not None:
            next(bg, None)
    if bg is not None:
        for _ in bg:
            pass

    sb.label = f"pj{which}"
    for c in range(NTC):
        for g2 in range(2):
            pbi = (2 * c + g2) % 4
            for p in range(NDC):
                sb.op("pe", lambda e, c=c, g2=g2, p=p, pbi=pbi: e.matmul(
                    pbank[pbi][:], ofm[p][:, c * 128:(c + 1) * 128],
                    Wp[p][:, g2 * 512:(g2 + 1) * 512],
                    start=(p == 0), stop=(p == NDC - 1)),
                    reads=[("ofm", p), ("Wp", p)], writes=[("pb", pbi)])
            j = (2 * c + g2) % 2
            sb.op("dve", lambda e, c=c, g2=g2, j=j, pbi=pbi: e.tensor_tensor(
                out=prtmp[j][:], in0=pbank[pbi][:], in1=x_tm[c][:, g2 * 512:(g2 + 1) * 512],
                op=ALU.add),
                reads=[("pb", pbi), ("x", c)], writes=[("prtmp", j)])
            sb.op("pool", lambda e, c=c, g2=g2, j=j: e.tensor_tensor(
                out=x_tm[c][:, g2 * 512:(g2 + 1) * 512], in0=prtmp[j][:],
                in1=bpro[:, g2 * 512:(g2 + 1) * 512], op=ALU.add),
                reads=[("prtmp", j), ("bpro",)], writes=[("x", c)])
    a2.close()


def wquant_gen(ctx, stack):
    nc, sb, dr, scr, C = ctx["nc"], ctx["sb"], ctx["dr"], ctx["scr"], ctx["C"]
    squad, pbank, sbuf = ctx["squad"], ctx["pbank"], ctx["sbuf"]
    QW = 1024
    wtmp = [sbuf(f"wq_tmp{i}", (128, QW), F32, stack=stack) for i in range(3)]
    zi = sbuf("wq_zi", (128, QW), I32, stack=stack)
    wqbf = sbuf("wq_bf", (128, QW), BF16, stack=stack)
    w1u = 4 * NDC
    w2u = NJC
    wsum = sbuf("wq_sum", (128, w1u + w2u), F32, stack=stack)
    wrow = sbuf("wq_row", (1, 64), F32, stack=stack)

    def w1_slice(u):
        dc, q = u // 4, u % 4
        return "w1T", dc * 128, q * QW, QW, 0, "w1qT_d", (dc, q)

    def w2_slice(u):
        return "w2T", u * 128, 0, D, 2, "w2qT_d", (u, 0)

    units = [w1_slice(u) for u in range(w1u)] + [w2_slice(u) for u in range(w2u)]
    for ui, (name, row0, cin, w, col, dst, dsl) in enumerate(units):
        i = ui % 3
        _lbl = sb.label
        sb.label = "wq"
        sb.op("sp", lambda e, name=name, row0=row0, cin=cin, w=w, i=i: e.dma_start(
            out=wtmp[i][:, 0:w], in_=dr[name][row0:row0 + 128, cin:cin + w]),
            writes=[("wqtmp", i)], dma=True)
        sb.label = _lbl
        yield
        sb.label = "wq"
        sb.op("dve", lambda e, ui=ui, i=i, w=w: e.tensor_reduce(
            out=wsum[:, ui:ui + 1], in_=wtmp[i][:, 0:w], axis=AX.X, op=ALU.add,
            apply_absolute_value=True),
            reads=[("wqtmp", i)], writes=[("wsum",)])
        sb.label = _lbl
        yield
    sb.label = "wq"
    sb.op("pe", lambda e: e.matmul(
        squad[0:1, 0:w1u + w2u], C["ones_row"][:, 0:1], wsum[:],
        start=True, stop=True),
        reads=[("ones_row",), ("wsum",)], writes=[("sq", 0)])
    sb.op("dve", lambda e: e.tensor_reduce(
        out=wrow[0:1, 0:1], in_=squad[0:1, 0:w1u], axis=AX.X, op=ALU.add),
        reads=[("sq", 0)], writes=[("wrow",)])
    sb.op("dve", lambda e: e.tensor_reduce(
        out=wrow[0:1, 1:2], in_=squad[0:1, w1u:w1u + w2u], axis=AX.X, op=ALU.add),
        reads=[("sq", 0)], writes=[("wrow",)])
    sb.op("dve", lambda e: e.tensor_scalar(
        out=wrow[0:1, 2:3], in0=wrow[0:1, 0:1], scalar1=1.0 / (DH * D),
        scalar2=1e-5, op0=ALU.mult, op1=ALU.max),
        reads=[("wrow",)], writes=[("wrow",)])
    sb.op("dve", lambda e: e.tensor_scalar(
        out=wrow[0:1, 3:4], in0=wrow[0:1, 1:2], scalar1=1.0 / (DH * D),
        scalar2=1e-5, op0=ALU.mult, op1=ALU.max),
        reads=[("wrow",)], writes=[("wrow",)])
    sb.op("pe", lambda e: e.matmul(
        squad[0:128, 0:2], C["ones_row"][0:1, :], wrow[0:1, 2:4],
        start=True, stop=True),
        reads=[("ones_row",), ("wrow",)], writes=[("sq", 0)])
    sb.op("dve", lambda e: e.reciprocal(out=C["wsc"][:, 0:1], in_=squad[:, 0:1]),
          reads=[("sq", 0)], writes=[("wsc",)])
    sb.op("dve", lambda e: e.tensor_scalar(
        out=C["wsc"][:, 1:2], in0=squad[:, 0:1], scalar1=1.0 / 127.0, scalar2=None,
        op0=ALU.mult),
        reads=[("sq", 0)], writes=[("wsc",)])
    sb.op("dve", lambda e: e.reciprocal(out=C["wsc"][:, 2:3], in_=squad[:, 1:2]),
          reads=[("sq", 0)], writes=[("wsc",)])
    sb.op("dve", lambda e: e.tensor_scalar(
        out=C["wsc"][:, 3:4], in0=squad[:, 1:2], scalar1=1.0 / 127.0, scalar2=None,
        op0=ALU.mult),
        reads=[("sq", 0)], writes=[("wsc",)])
    yield
    for ui, (name, row0, cin, w, col, dst, dsl) in enumerate(units):
        i = ui % 3
        _lbl = sb.label
        sb.label = "wq"
        sb.op("sp", lambda e, name=name, row0=row0, cin=cin, w=w, i=i: e.dma_start(
            out=wtmp[i][:, 0:w], in_=dr[name][row0:row0 + 128, cin:cin + w]),
            writes=[("wqtmp", i)], dma=True)
        yield
        sb.op("dve", lambda e, i=i, col=col, w=w: e.tensor_scalar(
            out=wtmp[i][:, 0:w], in0=wtmp[i][:, 0:w],
            scalar1=C["wsc"][:, col:col + 1], scalar2=1.0, op0=ALU.mult, op1=ALU.min),
            reads=[("wqtmp", i), ("wsc",)], writes=[("wqtmp", i)])
        yield
        sb.op("dve", lambda e, i=i, w=w: e.tensor_scalar(
            out=zi[:, 0:w], in0=wtmp[i][:, 0:w], scalar1=-1.0, scalar2=None,
            op0=ALU.max),
            reads=[("wqtmp", i)], writes=[("wqzi",)])
        yield
        sb.op("dve", lambda e, w=w: e.tensor_copy(out=wqbf[:, 0:w], in_=zi[:, 0:w]),
              reads=[("wqzi",)], writes=[("wqbf",)])
        yield
        if dst == "w1qT_d":
            dc, q4 = dsl
            sb.op("sp", lambda e, dc=dc, q4=q4, w=w: e.dma_start(
                out=scr["w1qT_d"][q4 * 8:q4 * 8 + 8, :, dc * 128:(dc + 1) * 128].rearrange("j p q -> p j q"),
                in_=wqbf[:, 0:w].rearrange("p (j q) -> p j q", q=128)),
                reads=[("wqbf",)], writes=[("w1qT_d", q4)], dma=True)
        else:
            u = dsl[0]
            for og in range(2):
                sb.op("sp", lambda e, u=u, og=og: e.dma_start(
                    out=scr["w2qT_d"][og, u // 2, :, u % 2, :],
                    in_=wqbf[:, og * 512:og * 512 + 512]),
                    reads=[("wqbf",)], writes=[("w2qT_d", u)], dma=True)
        sb.label = _lbl
        yield
    ctx["wquant_done"] = True


def emit_ffn(ctx, first):
    nc, sb, dr, scr, C = ctx["nc"], ctx["sb"], ctx["dr"], ctx["scr"], ctx["C"]
    x_tm, squad, pbank, sbuf = ctx["x_tm"], ctx["squad"], ctx["pbank"], ctx["sbuf"]
    tag = "B" if first else "DD"

    ph = ExitStack()
    tapw = sbuf(f"tapw_{tag}", (128, NJC * 3), F32, stack=ph)
    tapb = sbuf(f"tapb_{tag}", (128, NJC), F32, stack=ph)
    sb.op("sp", lambda e: e.dma_start(out=tapw[:], in_=dr["convw_t"]), writes=[("tapw",)], dma=True)
    sb.op("sp", lambda e: e.dma_start(out=tapb[:], in_=dr["convb_t"]), writes=[("tapb",)], dma=True)

    b1 = ExitStack()
    xq_fm = sbuf(f"xqfm_{tag}", (128, NDC * 2048), BF16, stack=ph)
    hsilu = sbuf(f"hsilu_{tag}", (128, NJC * 512), F32, stack=ph)
    amx = sbuf(f"amx_{tag}", (128, NTC), F32, stack=ph)
    sclt = sbuf(f"sclt_{tag}", (128, NTC), F32, stack=ph)
    r1t = sbuf(f"r1t_{tag}", (128, NTC), F32, stack=ph)
    s2tm = sbuf(f"s2tm_{tag}", (128, NTC), F32, stack=ph)
    r2t = sbuf(f"r2t_{tag}", (128, NTC), F32, stack=ph)
    rrow = sbuf(f"rrow_{tag}", (1, T + 2), F32, stack=ph)
    ln2f = [sbuf(f"ln2f_{tag}{i}", (128, D), F32, stack=b1) for i in range(2)]
    xqi = [sbuf(f"xqi_{tag}{i}", (128, D), I32, stack=b1) for i in range(2)]
    xqb = [sbuf(f"xqb_{tag}{i}", (128, D), BF16, stack=b1) for i in range(2)]

    emit_layernorm(sb, C, x_tm)
    for c in reversed(range(NTC)):
        i = c % 2
        sb.op("act", lambda e, c=c, i=i: e.activation(
            out=ln2f[i][:], in_=x_tm[c][:], func=AF.Identity,
            scale=C["rstd"][:, c:c + 1], bias=C["nmr"][:, c:c + 1]),
            reads=[("x", c), ("rstd",), ("nmr",)], writes=[("ln2f", i)])
        sb.op("dve", lambda e, c=c, i=i: e.tensor_reduce(
            out=amx[:, c:c + 1], in_=ln2f[i][:], axis=AX.X, op=ALU.max,
            apply_absolute_value=True),
            reads=[("ln2f", i)], writes=[("amx",)])
        sb.op("dve", lambda e, c=c: e.tensor_scalar(
            out=amx[:, c:c + 1], in0=amx[:, c:c + 1], scalar1=1e-5, scalar2=None,
            op0=ALU.max),
            reads=[("amx",)], writes=[("amx",)])
        sb.op("dve", lambda e, c=c: e.reciprocal(out=sclt[:, c:c + 1], in_=amx[:, c:c + 1]),
              reads=[("amx",)], writes=[("sclt",)])
        sb.op("dve", lambda e, c=c: e.tensor_scalar(
            out=sclt[:, c:c + 1], in0=sclt[:, c:c + 1], scalar1=127.0, scalar2=None,
            op0=ALU.mult),
            reads=[("sclt",)], writes=[("sclt",)])
        sb.op("dve", lambda e, c=c: e.tensor_scalar(
            out=r1t[:, c:c + 1], in0=amx[:, c:c + 1], scalar1=C["wsc"][:, 1:2],
            scalar2=None, op0=ALU.mult),
            reads=[("amx",), ("wsc",)], writes=[("r1t",)])
        sb.op("dve", lambda e, c=c, i=i: e.tensor_scalar(
            out=xqi[i][:], in0=ln2f[i][:], scalar1=sclt[:, c:c + 1], scalar2=None,
            op0=ALU.mult),
            reads=[("ln2f", i), ("sclt",)], writes=[("xqi", i)])
        sb.op("dve", lambda e, i=i: e.tensor_copy(out=xqb[i][:], in_=xqi[i][:]),
              reads=[("xqi", i)], writes=[("xqb", i)])
        tps = squad[:, 0:512].bitcast(BF16)
        for dc in range(NDC):
            sb.op("pe", lambda e, i=i, dc=dc, tps=tps: e.transpose(
                out=tps[:, dc * 128:(dc + 1) * 128],
                in_=xqb[i][:, dc * 128:(dc + 1) * 128], identity=C["eyeb"][:]),
                reads=[("xqb", i), ("eyeb",)], writes=[("sq", 0)])
        xv = xq_fm[:].rearrange("p (d t) -> p d t", t=2048)
        sb.op("dve", lambda e, c=c, tps=tps, xv=xv: e.tensor_copy(
            out=xv[:, :, c * 128:(c + 1) * 128],
            in_=tps[:].rearrange("p (d q) -> p d q", q=128)),
            reads=[("sq", 0)], writes=[("xqfm",)])
    if FFN_CUT == "b1":
        b1.close()
        ph.close()
        return
    r1T = sbuf(f"r1T_{tag}", (NTC, 128), F32, stack=b1)
    sb.op("pe", lambda e: e.transpose(
        squad[0:NTC, 0:128], r1t[:], identity=C["eyef"][:]),
        reads=[("r1t",), ("eyef",)], writes=[("sq", 0)])
    sb.op("dve", lambda e: e.tensor_copy(out=r1T[:], in_=squad[0:NTC, 0:128]),
          reads=[("sq", 0)], writes=[("r1T",)])
    sb.op("dve", lambda e: e.memset(rrow[:, 0:1], 0.0), writes=[("rrow",)])
    sb.op("dve", lambda e: e.memset(rrow[:, T + 1:T + 2], 0.0), writes=[("rrow",)])
    sb.op("sp", lambda e: e.dma_start(
        out=rrow[0:1, 1:T + 1].rearrange("o (t q) -> o t q", q=128),
        in_=r1T[:]),
        reads=[("r1T",)], writes=[("rrow",)], dma=True)
    b1.close()
    sb.barrier()
    if FFN_CUT == "b1r":
        ph.close()
        return

    b2 = ExitStack()
    wbuf = [sbuf(f"wbuf_{tag}{i}", (128, NDC * 128), BF16, stack=b2) for i in range(3)]
    w2buf = [sbuf(f"w2buf_{tag}{i}", (128, 1024), BF16, stack=b2) for i in range(2)]
    rbc = sbuf(f"rbc_{tag}", (128, 512), F32, stack=b2)
    rh1 = sbuf(f"rh1_{tag}", (128, 1), F32, stack=b2)
    hsv = sbuf(f"hsv_{tag}", (128, 2 * NJC), F32, stack=b2)
    hdeq = [sbuf(f"hdeq_{tag}{i}", (128, 514), F32, stack=b2) for i in range(2)]
    cvo = [sbuf(f"cvo_{tag}{i}", (128, 512), F32, stack=b2) for i in range(2)]
    ct0 = [sbuf(f"ct0_{tag}{i}", (128, 512), F32, stack=b2) for i in range(2)]
    hqi = sbuf(f"hqi_{tag}", (128, 512), I32, stack=b2)
    amac = sbuf(f"amac_{tag}", (128, 512), F32, stack=b2)
    stg = [sbuf(f"stg_{tag}{i}", (128, 1), F32, stack=b2) for i in range(2)]
    s2T = sbuf(f"s2T_{tag}", (1, 512), F32, stack=b2)
    for j in range(2):
        sb.op("dve", lambda e, j=j: e.memset(stg[j][:], 0.0), writes=[("stg", j)])
        sb.op("pool", lambda e, j=j: e.memset(hdeq[j][:, 513:514], 0.0), writes=[("hdeq", j)])
    hqb_all = hsilu[:].bitcast(BF16)

    def wload(jc):
        sb.op("sp", lambda e, jc=jc: e.dma_start(
            out=wbuf[jc % 3][:], in_=scr["w1qT_d"][jc]),
            reads=[("w1qT_d", jc // 8)], writes=[("wbuf", jc % 3)], dma=True)

    def w2load(jc, og, q):
        b = (jc // 2) % 2
        sb.op(q, lambda e, jc=jc, og=og, b=b: e.dma_start(
            out=w2buf[b][:].rearrange("p (j q) -> p j q", q=512),
            in_=scr["w2qT_d"][og, jc // 2]),
            reads=[("w2qT_d", jc), ("w2qT_d", jc + 1)], writes=[("w2b", b)], dma=True)

    def og_mms(tv, jc, og, dst):
        b = (jc // 2) % 2
        for s in range(4):
            sb.op("pe", lambda e, jc=jc, s=s, b=b, og=og, dst=dst: e.matmul(
                dst[s][:], hqb_all[:, jc * 1024 + s * 128:jc * 1024 + (s + 1) * 128],
                w2buf[b][:, (jc % 2) * 512:(jc % 2) * 512 + 512],
                start=(jc == 0), stop=(jc == NJC - 1)),
                reads=[("hsilu", jc), ("w2b", b)], writes=[("pb", s)])

    def og_res(tv, og):
        for s in range(4):
            c2 = tv * 4 + s
            lo = og * 512
            sb.op("dve", lambda e, s=s, c2=c2, lo=lo: e.scalar_tensor_tensor(
                out=x_tm[c2][:, lo:lo + 512], in0=pbank[s][:],
                scalar=r2t[:, c2:c2 + 1], in1=x_tm[c2][:, lo:lo + 512],
                op0=ALU.mult, op1=ALU.add),
                reads=[("pb", s), ("r2t",), ("x", c2)], writes=[("x", c2)])

    def emit_tail(tv):
        base = tv * 4
        for s in range(4):
            sb.op("pe", lambda e, s=s: e.transpose(
                pbank[1][:, s * 128:(s + 1) * 128], amac[:, s * 128:(s + 1) * 128],
                identity=C["eyef"][:]),
                reads=[("amac",), ("eyef",)], writes=[("pb", 1)])
        for s in range(4):
            c2 = base + s
            sb.op("dve", lambda e, s=s, c2=c2: e.tensor_reduce(
                out=s2tm[:, c2:c2 + 1], in_=pbank[1][:, s * 128:(s + 1) * 128],
                axis=AX.X, op=ALU.max),
                reads=[("pb", 1)], writes=[("s2tm",)])
        sb.op("dve", lambda e, base=base: e.tensor_scalar(
            out=s2tm[:, base:base + 4], in0=s2tm[:, base:base + 4], scalar1=1e-5,
            scalar2=None, op0=ALU.max),
            reads=[("s2tm",)], writes=[("s2tm",)])
        sb.op("dve", lambda e, base=base: e.tensor_scalar(
            out=r2t[:, base:base + 4], in0=s2tm[:, base:base + 4], scalar1=C["wsc"][:, 3:4],
            scalar2=None, op0=ALU.mult),
            reads=[("s2tm",), ("wsc",)], writes=[("r2t",)])
        sb.op("dve", lambda e, base=base: e.reciprocal(
            out=s2tm[:, base:base + 4], in_=s2tm[:, base:base + 4]),
            reads=[("s2tm",)], writes=[("s2tm",)])
        sb.op("dve", lambda e, base=base: e.tensor_scalar(
            out=s2tm[:, base:base + 4], in0=s2tm[:, base:base + 4], scalar1=127.0,
            scalar2=None, op0=ALU.mult),
            reads=[("s2tm",)], writes=[("s2tm",)])
        for s in range(4):
            j = s % 2
            c2 = base + s
            sb.op("dve", lambda e, j=j, c2=c2: e.tensor_copy(
                out=stg[j][:, 0:1], in_=s2tm[:, c2:c2 + 1]),
                reads=[("s2tm",)], writes=[("stg", j)])
            sb.op("pe", lambda e, j=j, s=s: e.transpose(
                pbank[1][0:1, s * 128:(s + 1) * 128], stg[j][:, 0:1], identity=C["eyef"][:]),
                reads=[("stg", j), ("eyef",)], writes=[("pb", 1)])
        sb.op("dve", lambda e: e.tensor_copy(out=s2T[0:1, :], in_=pbank[1][0:1, 0:512]),
              reads=[("pb", 1)], writes=[("s2T",)])
        for s in range(4):
            sb.op("pe", lambda e, s=s: e.matmul(
                squad[:, s * 128:(s + 1) * 128], C["ones_row"][0:1, :],
                s2T[0:1, s * 128:(s + 1) * 128],
                start=(s == 0), stop=(s == 3), skip_group_check=True),
                reads=[("ones_row",), ("s2T",)], writes=[("sq", 0)])

    tiles = [3] if FFN_CUT in ("t3", "jc1") else [3, 2, 1, 0]
    prev = None
    for tv in tiles:
        sb.label = f"ffn_{tag}t{tv}"
        t0 = tv * 512
        tokbase = t0 - 1 if tv > 0 else 0
        dst0 = 1 if tv == 0 else 0
        sb.op("pe", lambda e, tokbase=tokbase: e.matmul(
            squad[:, 0:512], C["ones_row"][0:1, :],
            rrow[0:1, tokbase + 1:tokbase + 513],
            start=True, stop=True),
            reads=[("ones_row",), ("rrow",)], writes=[("sq", 0)])
        sb.op("dve", lambda e: e.tensor_copy(out=rbc[:], in_=squad[:, 0:512]),
              reads=[("sq", 0)], writes=[("rbc",)])
        if tv == 3:
            sb.op("pe", lambda e: e.matmul(
                squad[:, 0:1], C["ones_row"][0:1, :], rrow[0:1, 2048:2049],
                start=True, stop=True),
                reads=[("ones_row",), ("rrow",)], writes=[("sq", 0)])
            sb.op("dve", lambda e: e.tensor_copy(out=rh1[:], in_=squad[:, 0:1]),
                  reads=[("sq", 0)], writes=[("rh1",)])
        for pf in range(3):
            wload(pf)
        def front(jc):
            i = jc % 2
            bank = jc % 4
            if tv == 0:
                sb.op("pool", lambda e, i=i: e.memset(hdeq[i][:, 0:1], 0.0),
                      writes=[("hdeq", i)])
            sb.op("dve", lambda e, i=i, bank=bank, dst0=dst0: e.tensor_tensor(
                out=hdeq[i][:, dst0:dst0 + 512], in0=squad[:, bank * 512:bank * 512 + 512],
                in1=rbc[:], op=ALU.mult),
                reads=[("sq", bank), ("rbc",)], writes=[("hdeq", i)])
            if tv == 3:
                hb = jc % 4
                sb.op("dve", lambda e, i=i, hb=hb: e.tensor_tensor(
                    out=hdeq[i][:, 512:513], in0=pbank[hb][:, 0:1], in1=rh1[:],
                    op=ALU.mult),
                    reads=[("pb", hb), ("rh1",)], writes=[("hdeq", i)])
            elif tv > 0:
                sb.op("pool", lambda e, i=i, jc=jc: e.tensor_copy(
                    out=hdeq[i][:, 512:514], in_=hsv[:, 2 * jc:2 * jc + 2]),
                    reads=[("hsv", jc)], writes=[("hdeq", i)])
            else:
                sb.op("pool", lambda e, i=i, jc=jc: e.tensor_copy(
                    out=hdeq[i][:, 513:514], in_=hsv[:, 2 * jc + 1:2 * jc + 2]),
                    reads=[("hsv", jc)], writes=[("hdeq", i)])
            if tv > 0:
                sb.op("pool", lambda e, i=i, jc=jc: e.tensor_copy(
                    out=hsv[:, 2 * jc:2 * jc + 2], in_=hdeq[i][:, 0:2]),
                    reads=[("hdeq", i)], writes=[("hsv", jc)])
            sb.op("act", lambda e, i=i, jc=jc: e.activation(
                out=cvo[i][:], in_=hdeq[i][:, 2:514], func=AF.Identity,
                scale=tapw[:, 3 * jc + 2:3 * jc + 3], bias=tapb[:, jc:jc + 1]),
                reads=[("hdeq", i), ("tapw",), ("tapb",)], writes=[("cvo", i)])
            sb.op("act", lambda e, i=i, jc=jc: e.activation(
                out=ct0[i][:, 256:512], in_=hdeq[i][:, 256:512], func=AF.Identity,
                scale=tapw[:, 3 * jc:3 * jc + 1], bias=0.0),
                reads=[("hdeq", i), ("tapw",)], writes=[("ct0", i)])
            sb.op("dve", lambda e, i=i, jc=jc: e.tensor_scalar(
                out=ct0[i][:, 0:256], in0=hdeq[i][:, 0:256],
                scalar1=tapw[:, 3 * jc:3 * jc + 1], scalar2=None, op0=ALU.mult),
                reads=[("hdeq", i), ("tapw",)], writes=[("ct0", i)])
            sb.op("dve", lambda e, i=i, jc=jc: e.scalar_tensor_tensor(
                out=cvo[i][:], in0=hdeq[i][:, 1:513], scalar=tapw[:, 3 * jc + 1:3 * jc + 2],
                in1=cvo[i][:], op0=ALU.mult, op1=ALU.add),
                reads=[("hdeq", i), ("tapw",), ("cvo", i)], writes=[("cvo", i)])

        def back_pool(jc):
            i = jc % 2
            sb.op("pool", lambda e, i=i: e.tensor_tensor(
                out=cvo[i][:], in0=cvo[i][:], in1=ct0[i][:], op=ALU.add),
                reads=[("cvo", i), ("ct0", i)], writes=[("cvo", i)])

        def back(jc):
            i = jc % 2
            sb.op("act", lambda e, i=i, jc=jc: e.activation(
                out=hsilu[:, jc * 512:(jc + 1) * 512], in_=cvo[i][:], func=AF.Silu),
                reads=[("cvo", i)], writes=[("hsilu", jc)])
            sb.op("act", lambda e, i=i, jc=jc: e.activation(
                out=ct0[i][:], in_=hsilu[:, jc * 512:(jc + 1) * 512], func=AF.Abs),
                reads=[("hsilu", jc), ("cvo", i), ("ct0", i)], writes=[("ct0", i)])
            if jc == 0:
                sb.op("dve", lambda e, i=i: e.tensor_copy(out=amac[:], in_=ct0[i][:]),
                      reads=[("ct0", i)], writes=[("amac",)])
            else:
                sb.op("dve", lambda e, i=i: e.tensor_tensor(
                    out=amac[:], in0=amac[:], in1=ct0[i][:], op=ALU.max),
                    reads=[("amac",), ("ct0", i)], writes=[("amac",)])

        for jc in range(NJC):
            sb.label = f"ffn_{tag}t{tv}_jc{jc}"
            bank = jc % 4
            if prev is not None:
                if jc == 0:
                    w2load(0, 1, "sp")
                if jc % 2 == 0 and jc + 2 < NJC:
                    w2load(jc + 2, 1, "sp")
                og_mms(prev, jc, 1, pbank)
            for dc in range(NDC):
                sb.op("pe", lambda e, jc=jc, dc=dc, bank=bank, tokbase=tokbase: e.matmul(
                    squad[:, bank * 512:bank * 512 + 512],
                    wbuf[jc % 3][:, dc * 128:(dc + 1) * 128],
                    xq_fm[:, dc * 2048 + tokbase:dc * 2048 + tokbase + 512],
                    start=(dc == 0), stop=(dc == NDC - 1)),
                    reads=[("wbuf", jc % 3), ("xqfm",)], writes=[("sq", bank)])
            if tv == 3:
                hb = jc % 4
                for dc in range(NDC):
                    sb.op("pe", lambda e, jc=jc, dc=dc, hb=hb: e.matmul(
                        pbank[hb][:, 0:1],
                        wbuf[jc % 3][:, dc * 128:(dc + 1) * 128],
                        xq_fm[:, dc * 2048 + 2047:dc * 2048 + 2048],
                        start=(dc == 0), stop=(dc == NDC - 1)),
                        reads=[("wbuf", jc % 3), ("xqfm",)], writes=[("pb", hb)])
            if jc + 3 < NJC:
                wload(jc + 3)
            if jc > 0:
                back_pool(jc - 1)
            front(jc)
            if jc > 0:
                back(jc - 1)
            if FFN_CUT == "jc1":
                b2.close()
                ph.close()
                return
        back_pool(NJC - 1)
        back(NJC - 1)
        if prev is not None:
            og_res(prev, 1)
        sb.label = f"ffn_{tag}t{tv}_tl"
        emit_tail(tv)
        sb.label = f"ffn_{tag}t{tv}_q"
        for jc in range(NJC):
            sb.op("dve", lambda e, jc=jc: e.tensor_tensor(
                out=hqi[:], in0=hsilu[:, jc * 512:(jc + 1) * 512], in1=squad[:, 0:512],
                op=ALU.mult),
                reads=[("hsilu", jc), ("sq", 0)], writes=[("hqi",)])
            sb.op("dve", lambda e, jc=jc: e.tensor_copy(
                out=hqb_all[:, jc * 1024:jc * 1024 + 512], in_=hqi[:]),
                reads=[("hqi",)], writes=[("hsilu", jc)])
            if jc == 0:
                w2load(0, 0, "sp")
            if jc % 2 == 0 and jc + 2 < NJC:
                w2load(jc + 2, 0, "sp")
            og_mms(tv, jc, 0, pbank)
        og_res(tv, 0)
        prev = tv
    if FFN_CUT in ("t3", "jc1"):
        b2.close()
        ph.close()
        return
    sb.label = f"ffn_{tag}fl"
    for jc in range(NJC):
        if jc == 0:
            w2load(0, 1, "sp")
        if jc % 2 == 0 and jc + 2 < NJC:
            w2load(jc + 2, 1, "sp")
        og_mms(0, jc, 1, pbank)
    og_res(0, 1)
    b2.close()
    ph.close()



def _host_constants():
    pos = np.arange(T, dtype=np.float64)
    i = np.arange(32, dtype=np.float64)
    inv = 10000.0 ** (-i / 32.0)
    theta = pos[:, None] * inv[None, :]
    cos = np.cos(theta).astype(np.float32)
    sin = np.sin(theta).astype(np.float32)
    cos_sb = np.ascontiguousarray(cos.reshape(NTC, 128, 32).transpose(1, 0, 2))
    sin_sb = np.ascontiguousarray(sin.reshape(NTC, 128, 32).transpose(1, 0, 2))
    import ml_dtypes
    cext = np.broadcast_to(cos_sb[:, :, None, :], (128, NTC, 2, 32))
    cext_sb = np.ascontiguousarray(cext.reshape(128, NTC * 64)).astype(ml_dtypes.bfloat16)
    sext = np.stack([-sin_sb, sin_sb], axis=2)
    sext_sb = np.ascontiguousarray(sext.reshape(128, NTC * 64)).astype(ml_dtypes.bfloat16)
    kt = np.arange(128)
    tri = (kt[:, None] <= kt[None, :]).astype(np.float32)
    eye_bf = np.eye(128, dtype=ml_dtypes.bfloat16)
    eye_f32 = np.eye(128, dtype=np.float32)
    return dict(cext_sb=cext_sb, sext_sb=sext_sb, tri=tri, eye_bf=eye_bf, eye_f32=eye_f32)


def kernel(L0_attn_a_x, L0_attn_a_W_attn, L0_attn_a_b_attn, L0_attn_a_W_proj,
           L0_attn_a_b_proj, L0_attn_b_qkv_w, L0_attn_b_qkv_b, L0_attn_b_out_w,
           L0_attn_b_out_b, L0_ffn_w1, L0_ffn_conv_w, L0_ffn_conv_b, L0_ffn_w2):
    f32 = lambda a: np.ascontiguousarray(np.asarray(a), dtype=np.float32)
    x = f32(L0_attn_a_x)
    shared = dict(
        W_attn=f32(L0_attn_a_W_attn),
        b_attn=f32(L0_attn_a_b_attn),
        W_proj=f32(L0_attn_a_W_proj),
        b_proj=f32(L0_attn_a_b_proj),
        qkv_wT=f32(np.asarray(L0_attn_b_qkv_w).T),
        qkv_b=f32(L0_attn_b_qkv_b),
        out_wT=f32(np.asarray(L0_attn_b_out_w).T),
        out_b=f32(L0_attn_b_out_b),
        w1T=f32(np.asarray(L0_ffn_w1).T),
        convw_t=f32(np.asarray(L0_ffn_conv_w).reshape(NJC, 128, 3).transpose(1, 0, 2).reshape(128, NJC * 3)),
        convb_t=f32(np.asarray(L0_ffn_conv_b).reshape(NJC, 128).T),
        w2T=f32(np.asarray(L0_ffn_w2).T),
        **_host_constants(),
    )
    nc = build_program()
    in_maps = [dict(shared, x=x[b]) for b in range(8)]
    res = run_bass_kernel_spmd(nc, in_maps, core_ids=list(range(8)))
    return np.stack([res.results[b]["out"] for b in range(8)], axis=0)


# version 24
# speedup vs baseline: 1.0502x; 1.0502x over previous
import numpy as np
from contextlib import ExitStack

import concourse.bass as bass
import concourse.mybir as mybir
from concourse.bass_utils import run_bass_kernel_spmd

F32 = mybir.dt.float32
BF16 = mybir.dt.bfloat16
I32 = mybir.dt.int32
AF = mybir.ActivationFunctionType
ALU = mybir.AluOpType
AX = mybir.AxisListType

T = 2048
D = 1024
H = 16
HD = 64
NTC = T // 128
NDC = D // 128
DH = 4096
NJC = DH // 128
EPS_LN = 1e-5

PHASES = ("A", "B", "C", "D")
FFN_CUT = None

ENGINES = ("pe", "dve", "act", "pool", "sp", "gp")
STREAM_OF = {"pe": "pe", "dve": "dve", "act": "act", "pool": "gpq", "gp": "gpq", "sp": "sp"}
NO_SELF_EDGE = {"pe", "sp", "gp"}


class SB:

    NLANE = 8

    def __init__(self, nc):
        self.nc = nc
        self.streams = {"pe": [], "dve": [], "act": [], "gpq": [], "sp": []}
        self.count = {e: 0 for e in ("pe", "dve", "act", "pool")}
        self.qcount = {"sp": 0, "gp": 0}
        self.last_w = {}
        self.readers = {}
        self.waited = {}
        self.sems = {}
        self.label = ""

    def _waited(self, eng):
        return self.waited.setdefault(eng, {})

    def op(self, eng, fn, reads=(), writes=(), dma=False):
        need = {}
        for k in reads:
            w = self.last_w.get(k)
            if w is not None:
                need[w[0]] = max(need.get(w[0], 0), w[1])
        for k in writes:
            w = self.last_w.get(k)
            if w is not None:
                need[w[0]] = max(need.get(w[0], 0), w[1])
            for f, c in self.readers.get(k, {}).items():
                need[f] = max(need.get(f, 0), c)
        if dma:
            q = eng
            slot = self.qcount[q] % self.NLANE
            cycle = self.qcount[q] // self.NLANE
            self.qcount[q] += 1
            pid = (q, slot)
            if cycle > 0:
                need[pid] = max(need.get(pid, 0), 16 * cycle)
            my_id, my_cnt = pid, 16 * (cycle + 1)
            wm = self._waited(q)
        else:
            self.count[eng] += 1
            my_id, my_cnt = eng, self.count[eng]
            wm = self._waited(eng)
        waits = []
        for f, c in sorted(need.items(), key=str):
            if f == eng and eng in NO_SELF_EDGE:
                continue
            if c > wm.get(f, 0):
                waits.append((f, c))
                wm[f] = c
        sems = self.sems
        step = 16 if dma else 1
        stream = STREAM_OF[eng]

        lbl = self.label

        def emit(engobj, waits=waits, fn=fn, my_id=my_id, step=step, lbl=lbl):
            for f, c in waits:
                engobj.wait_ge(sems[f], c)
            ins = fn(engobj)
            if lbl:
                ins.annotate(lbl)
            ins.then_inc(sems[my_id], step)

        self.streams[stream].append(emit)
        for k in writes:
            self.last_w[k] = (my_id, my_cnt)
            self.readers[k] = {}
        for k in reads:
            self.readers.setdefault(k, {})[my_id] = my_cnt
        return my_cnt

    def _all_producers(self):
        cur = {e: c for e, c in self.count.items() if c > 0}
        for q in ("sp", "gp"):
            n = self.qcount[q]
            for slot in range(self.NLANE):
                uses = (n - slot + self.NLANE - 1) // self.NLANE
                if uses > 0:
                    cur[(q, slot)] = 16 * uses
        return cur

    def barrier(self):
        cur = self._all_producers()
        sems = self.sems
        for e in ("pe", "dve", "act", "pool", "sp", "gp"):
            wm = self._waited(e)
            waits = []
            for f, c in cur.items():
                if f == e:
                    continue
                if c > wm.get(f, 0):
                    waits.append((f, c))
                    wm[f] = c
            if not waits:
                continue

            def emit(engobj, waits=waits):
                for f, c in waits:
                    engobj.wait_ge(sems[f], c)

            self.streams[STREAM_OF[e]].append(emit)

    def finish(self):
        cur = self._all_producers()
        sems = self.sems

        def emit(engobj):
            for f, c in cur.items():
                engobj.wait_ge(sems[f], c)

        self.streams["sp"].append(emit)


def build_program():
    nc = bass.Bass("TRN2", target_bir_lowering=False, debug=False)
    sb = SB(nc)

    dr = {}
    def din(name, shape, dt=F32):
        dr[name] = nc.dram_tensor(name, list(shape), dt, kind="ExternalInput").ap()

    din("x", (T, D))
    din("W_attn", (D, 3 * D))
    din("b_attn", (3 * D,))
    din("W_proj", (D, D))
    din("b_proj", (D,))
    din("qkv_wT", (D, 3 * D))
    din("qkv_b", (3 * D,))
    din("out_wT", (D, D))
    din("out_b", (D,))
    din("w1T", (D, DH))
    din("convw_t", (128, NJC * 3))
    din("convb_t", (128, NJC))
    din("w2T", (DH, D))
    din("cext_sb", (128, NTC * 64), BF16)
    din("sext_sb", (128, NTC * 64), BF16)
    din("tri", (128, 128))
    din("eye_bf", (128, 128), BF16)
    din("eye_f32", (128, 128))

    out = nc.dram_tensor("out", [T, D], F32, kind="ExternalOutput").ap()

    scr = {}
    def dscr(name, shape, dt):
        scr[name] = nc.dram_tensor(name, list(shape), dt, kind="Internal").ap()

    dscr("qspill", (NTC, NDC, 128, 128), BF16)
    dscr("kspill", (NTC, NDC, 128, 128), BF16)
    dscr("vspill", (NTC, 128, H * 65), BF16)
    dscr("w1qT_d", (NJC, 128, D), BF16)
    dscr("w2qT_d", (2, NJC // 2, 128, 2, 512), BF16)

    with ExitStack() as st:
        for e in ("pe", "dve", "act", "pool"):
            sb.sems[e] = st.enter_context(nc.semaphore(f"sem_{e}"))
        for q in ("sp", "gp"):
            for slot in range(SB.NLANE):
                sb.sems[(q, slot)] = st.enter_context(nc.semaphore(f"sem_{q}{slot}"))

        squad = st.enter_context(nc.psum_tensor("squad", [128, 2048], F32))
        pbank = [st.enter_context(nc.psum_tensor(f"pb{i}", [128, 512], F32)) for i in range(4)]

        x_tm = [st.enter_context(nc.sbuf_tensor(f"x{c}", [128, D], F32)) for c in range(NTC)]

        ctx = dict(nc=nc, sb=sb, dr=dr, scr=scr, out=out, x_tm=x_tm,
                   squad=squad, pbank=pbank, st=st)
        emit_all(ctx)
        sb.finish()

        block = st.enter_context(nc.Block())

        @block.sync
        def _(sync):
            for f in sb.streams["sp"]:
                f(sync)

        @block.gpsimd
        def _(g):
            for f in sb.streams["gpq"]:
                f(g)

        @block.vector
        def _(v):
            for f in sb.streams["dve"]:
                f(v)

        @block.scalar
        def _(s):
            for f in sb.streams["act"]:
                f(s)

        @block.tensor
        def _(t):
            for f in sb.streams["pe"]:
                f(t)

    return nc



def emit_all(ctx):
    nc, sb, dr, st = ctx["nc"], ctx["sb"], ctx["dr"], ctx["st"]
    x_tm, out = ctx["x_tm"], ctx["out"]

    def sbuf(name, shape, dt, stack=st):
        return stack.enter_context(nc.sbuf_tensor(name, list(shape), dt))
    ctx["sbuf"] = sbuf

    C = {}
    ctx["C"] = C
    C["tri"] = sbuf("tri_s", (128, 128), F32)
    C["eyeb"] = sbuf("eyeb_s", (128, 128), BF16)
    C["eyef"] = sbuf("eyef_s", (128, 128), F32)
    C["ones_row"] = sbuf("ones_row", (128, 128), F32)
    C["Cext"] = sbuf("Cext", (128, NTC * 64), BF16)
    C["Sext"] = sbuf("Sext", (128, NTC * 64), BF16)
    C["mv"] = sbuf("mv_all", (128, 2 * NTC), F32)
    C["stat6"] = sbuf("stat6", (128, 12), F32)
    C["rstd"] = sbuf("rstd_all", (128, NTC), F32)
    C["nmr"] = sbuf("nmr_all", (128, NTC), F32)
    C["lnv"] = sbuf("lnv_all", (128, NTC), F32)
    C["wsc"] = sbuf("wsc_all", (128, 4), F32)

    for nm, src in (("tri", "tri"), ("eyeb", "eye_bf"), ("eyef", "eye_f32"),
                    ("Cext", "cext_sb"), ("Sext", "sext_sb")):
        sb.op("sp", lambda e, nm=nm, src=src: e.dma_start(out=C[nm][:], in_=dr[src]),
              writes=[(nm,)], dma=True)
    sb.op("dve", lambda e: e.memset(C["ones_row"][:], 1.0), writes=[("ones_row",)])
    C["k"] = sbuf("kconst", (128, 8), F32)
    for col, val in ((0, EPS_LN), (1, -0.5), (2, 0.125), (3, -1.0), (4, 127.0), (5, 1e-12)):
        sb.op("dve", lambda e, col=col, val=val: e.memset(C["k"][:, col:col + 1], val),
              writes=[("k",)])

    for c in range(NTC):
        sb.op("sp", lambda e, c=c: e.dma_start(
            out=x_tm[c][:], in_=dr["x"][c * 128:(c + 1) * 128, :]),
            writes=[("x", c)], dma=True)

    if "A" in PHASES:
        wq_stack = ExitStack()
        if "B" in PHASES or "D" in PHASES:
            bg = wquant_gen(ctx, wq_stack)
            next(bg, None)
        else:
            bg = None
        emit_attention(ctx, which="a", bg=bg)
        wq_stack.close()
    if "B" in PHASES:
        sb.barrier()
        emit_ffn(ctx, first=True)
    if "C" in PHASES:
        sb.barrier()
        emit_attention(ctx, which="b")
    if "D" in PHASES:
        sb.barrier()
        emit_ffn(ctx, first=False)

    sb.barrier()
    for c in range(NTC):
        sb.op("sp", lambda e, c=c: e.dma_start(
            out=out[c * 128:(c + 1) * 128, :], in_=x_tm[c][:]),
            reads=[("x", c)], dma=True)


def emit_layernorm(sb, C, x_tm):
    for c in range(NTC):
        sb.op("dve", lambda e, c=c: e.bn_stats(out=C["stat6"][:, 0:6], in_=x_tm[c][:, 0:512]),
              reads=[("x", c)], writes=[("stat6",)])
        sb.op("dve", lambda e, c=c: e.bn_stats(out=C["stat6"][:, 6:12], in_=x_tm[c][:, 512:1024]),
              reads=[("x", c)], writes=[("stat6",)])
        sb.op("dve", lambda e, c=c: e.bn_aggr(out=C["mv"][:, 2 * c:2 * c + 2], in_=C["stat6"][:]),
              reads=[("stat6",)], writes=[("mv",)])
    mvv = C["mv"][:].rearrange("p (c two) -> p c two", two=2)
    sb.op("act", lambda e: e.activation(out=C["lnv"][:], in_=mvv[:, :, 1], func=AF.Ln, bias=C["k"][:, 0:1]),
          reads=[("mv",)], writes=[("lnv",)])
    sb.op("act", lambda e: e.activation(out=C["rstd"][:], in_=C["lnv"][:], func=AF.Exp, scale=C["k"][:, 1:2]),
          reads=[("lnv",), ("k",)], writes=[("rstd",)])
    sb.op("dve", lambda e: e.scalar_tensor_tensor(
        out=C["nmr"][:], in0=mvv[:, :, 0], scalar=-1.0, in1=C["rstd"][:],
        op0=ALU.mult, op1=ALU.mult),
        reads=[("mv",), ("rstd",)], writes=[("nmr",)])


def emit_attention(ctx, which, bg=None):
    nc, sb, dr, scr, C = ctx["nc"], ctx["sb"], ctx["dr"], ctx["scr"], ctx["C"]
    x_tm, squad, pbank, sbuf = ctx["x_tm"], ctx["squad"], ctx["pbank"], ctx["sbuf"]
    causal = (which == "a")
    wname = "W_attn" if causal else "qkv_wT"
    bname = "b_attn" if causal else "qkv_b"
    pname = "W_proj" if causal else "out_wT"
    pbias = "b_proj" if causal else "out_b"

    s1 = ExitStack()
    vstage = [sbuf(f"vstage{which}_{i}", (128, H * 65), BF16, stack=s1) for i in range(2)]
    Wqkv = [sbuf(f"Wqkv{which}_{dc}", (128, 3 * D), BF16, stack=s1) for dc in range(NDC)]
    battrow = sbuf(f"battrow_{which}", (1, 3 * D), BF16, stack=s1)
    onesbf = sbuf(f"onesbf_{which}", (1, 128), BF16, stack=s1)
    ln = [sbuf(f"ln{which}_{i}", (128, D), BF16, stack=s1) for i in range(2)]
    lnT = [sbuf(f"lnT{which}_{i}", (128, D), BF16, stack=s1) for i in range(2)]
    qkbf = [sbuf(f"qkbf{which}_{i}", (128, 2048), BF16, stack=s1) for i in range(2)]
    ropet = [sbuf(f"ropet{which}_{i}", (128, 2048), BF16, stack=s1) for i in range(2)]
    ropeq = [sbuf(f"ropeq{which}_{i}", (128, 2048), BF16, stack=s1) for i in range(2)]
    qstage = sbuf(f"qstage_{which}", (128, 1024), BF16, stack=s1)
    kstage = sbuf(f"kstage_{which}", (128, 1024), BF16, stack=s1)
    if not causal:
        s1tmp = sbuf(f"s1tmp_{which}", (128, 1024), F32, stack=s1)
        s1b = sbuf(f"s1b_{which}", (128, 128), F32, stack=s1)

    for dc in range(NDC):
        sb.op("gp", lambda e, dc=dc: e.dma_start(
            out=Wqkv[dc][:], in_=dr[wname][dc * 128:(dc + 1) * 128, :]),
            writes=[("Wqkv", dc)], dma=True)
    sb.op("gp", lambda e: e.dma_start(out=battrow[:], in_=dr[bname][None, :]),
          writes=[("battrow",)], dma=True)
    sb.op("dve", lambda e: e.memset(onesbf[:], 1.0), writes=[("onesbf",)])

    for i in range(2):
        sb.op("dve", lambda e, i=i: e.memset(
            vstage[i][:].rearrange("p (h c65) -> p h c65", c65=65)[:, :, 64:65], 1.0),
            writes=[("vstage", i)])

    emit_layernorm(sb, C, x_tm)

    pending = [None]

    for c in range(NTC):
        sb.label = f"s1{which}_c{c}"
        i = c % 2
        sb.op("act", lambda e, c=c, i=i: e.activation(
            out=ln[i][:], in_=x_tm[c][:], func=AF.Identity,
            scale=C["rstd"][:, c:c + 1], bias=C["nmr"][:, c:c + 1]),
            reads=[("x", c), ("rstd",), ("nmr",)], writes=[("ln", i)])
        tps = squad[:, 0:512].bitcast(BF16)
        for dc in range(NDC):
            sb.op("pe", lambda e, i=i, dc=dc, tps=tps: e.transpose(
                out=tps[:, dc * 128:(dc + 1) * 128],
                in_=ln[i][:, dc * 128:(dc + 1) * 128], identity=C["eyeb"][:]),
                reads=[("ln", i), ("eyeb",)], writes=[("sq", 0)])
        sb.op("dve", lambda e, i=i, tps=tps: e.tensor_copy(out=lnT[i][:], in_=tps),
              reads=[("sq", 0)], writes=[("lnT", i)])
        for g in range(6):
            pb = pbank[g % 2]
            for dc in range(NDC):
                sb.op("pe", lambda e, i=i, dc=dc, g=g, pb=pb: e.matmul(
                    pb[:], lnT[i][:, dc * 128:(dc + 1) * 128],
                    Wqkv[dc][:, g * 512:(g + 1) * 512],
                    start=(dc == 0), stop=False),
                    reads=[("lnT", i), ("Wqkv", dc)], writes=[("pb", g % 2)])
            sb.op("pe", lambda e, g=g, pb=pb: e.matmul(
                pb[:], onesbf[0:1, :], battrow[0:1, g * 512:(g + 1) * 512],
                start=False, stop=True),
                reads=[("onesbf",), ("battrow",)], writes=[("pb", g % 2)])
            if g < 4:
                if causal:
                    dst = qkbf[i][:, g * 512:(g + 1) * 512]
                    sb.op("act", lambda e, dst=dst, pb=pb: e.activation(
                        out=dst, in_=pb[:], func=AF.Identity),
                        reads=[("pb", g % 2)], writes=[("qkbf", i)])
                else:
                    dst = qkbf[i][:, g * 512:(g + 1) * 512].rearrange(
                        "p (h b a) -> p h a b", b=2, a=32)
                    src0 = pb[:].rearrange("p (h a b) -> p h a b", a=32, b=2)
                    sb.op("act", lambda e, dst=dst, src0=src0: e.activation(
                        out=dst, in_=src0, func=AF.Identity),
                        reads=[("pb", g % 2)], writes=[("qkbf", i)])
            else:
                h0 = (g - 4) * 8
                vv = vstage[i][:].rearrange("p (h c65) -> p h c65", c65=65)[:, h0:h0 + 8, 0:64]
                sb.op("act", lambda e, pb=pb, vv=vv: e.activation(
                    out=vv, in_=pb[:].rearrange("p (h q) -> p h q", q=64), func=AF.Identity),
                    reads=[("pb", g % 2)], writes=[("vstage", i)])
        if pending[0] is not None:
            pending[0]()
        qv = qkbf[i][:].rearrange("p (g2 h i2) -> p g2 h i2", h=2, i2=32)
        cs = C["Cext"][:, c * 64:(c + 1) * 64].rearrange("p (h i2) -> p h i2", h=2)[:, None, :, :].broadcast_to((128, 32, 2, 32))
        sx = C["Sext"][:, c * 64:(c + 1) * 64].rearrange("p (h i2) -> p h i2", h=2)[:, None, :, :].broadcast_to((128, 32, 2, 32))
        rt = ropet[i][:].rearrange("p (g2 h i2) -> p g2 h i2", h=2, i2=32)
        rq = ropeq[i][:].rearrange("p (g2 h i2) -> p g2 h i2", h=2, i2=32)
        sb.op("dve", lambda e, qv=qv, sx=sx, rt=rt, i=i: e.tensor_tensor(
            out=rt[:, :, 0, :], in0=qv[:, :, 1, :], in1=sx[:, :, 0, :], op=ALU.mult),
            reads=[("qkbf", i), ("Sext",)], writes=[("ropet", i)])
        sb.op("dve", lambda e, qv=qv, sx=sx, rt=rt, i=i: e.tensor_tensor(
            out=rt[:, :, 1, :], in0=qv[:, :, 0, :], in1=sx[:, :, 1, :], op=ALU.mult),
            reads=[("qkbf", i), ("Sext",)], writes=[("ropet", i)])
        sb.op("dve", lambda e, qv=qv, cs=cs, rq=rq, i=i: e.tensor_tensor(
            out=rq, in0=qv, in1=cs, op=ALU.mult),
            reads=[("qkbf", i), ("Cext",)], writes=[("ropeq", i)])
        sb.op("dve", lambda e, i=i: e.tensor_tensor(
            out=ropeq[i][:], in0=ropeq[i][:], in1=ropet[i][:], op=ALU.add),
            reads=[("ropet", i), ("ropeq", i)], writes=[("ropeq", i)])
        if not causal:
            rv = ropeq[i][:].rearrange("p (g2 q) -> p g2 q", q=64)
            for half in range(2):
                sb.op("pool", lambda e, rv=rv, half=half: e.tensor_tensor(
                    out=s1tmp[:].rearrange("p (g2 q) -> p g2 q", q=32),
                    in0=rv[:, :, half * 32:(half + 1) * 32],
                    in1=rv[:, :, half * 32:(half + 1) * 32], op=ALU.mult),
                    reads=[("ropeq", i)], writes=[("s1tmp",)])
                sb.op("dve", lambda e, half=half: e.tensor_reduce(
                    out=s1b[:, half * 32:(half + 1) * 32],
                    in_=s1tmp[:].rearrange("p (g2 q) -> p g2 q", q=32),
                    axis=AX.X, op=ALU.add),
                    reads=[("s1tmp",)], writes=[("s1b",)])
            sb.op("dve", lambda e: e.tensor_tensor(
                out=s1b[:, 64:96], in0=s1b[:, 0:32], in1=s1b[:, 32:64], op=ALU.add),
                reads=[("s1b",)], writes=[("s1b",)])
            sb.op("act", lambda e: e.activation(out=s1b[:, 96:128], in_=s1b[:, 64:96], func=AF.Ln),
                  reads=[("s1b",)], writes=[("s1b",)])
            sb.op("act", lambda e: e.activation(out=s1b[:, 64:96], in_=s1b[:, 96:128], func=AF.Exp, scale=C["k"][:, 1:2]),
                  reads=[("s1b",), ("k",)], writes=[("s1b",)])
            sb.op("dve", lambda e: e.tensor_scalar(
                out=s1b[:, 64:96], in0=s1b[:, 64:96], scalar1=1e12, scalar2=None, op0=ALU.min),
                reads=[("s1b",)], writes=[("s1b",)])
            sb.op("dve", lambda e, rv=rv: e.tensor_tensor(
                out=rv, in0=rv, in1=s1b[:, 64:96][:, :, None].broadcast_to((128, 32, 64)),
                op=ALU.mult),
                reads=[("ropeq", i), ("s1b",)], writes=[("ropeq", i)])

        def mk_pending(c=c, i=i):
            def emit_tail():
                tq = squad[:, 512:1024].bitcast(BF16)
                tk = squad[:, 1024:1536].bitcast(BF16)
                for dc in range(NDC):
                    sb.op("pe", lambda e, i=i, dc=dc, tq=tq: e.transpose(
                        out=tq[:, dc * 128:(dc + 1) * 128],
                        in_=ropeq[i][:, dc * 128:(dc + 1) * 128], identity=C["eyeb"][:]),
                        reads=[("ropeq", i), ("eyeb",)], writes=[("sq", 1)])
                    sb.op("pe", lambda e, i=i, dc=dc, tk=tk: e.transpose(
                        out=tk[:, dc * 128:(dc + 1) * 128],
                        in_=ropeq[i][:, 1024 + dc * 128:1024 + (dc + 1) * 128], identity=C["eyeb"][:]),
                        reads=[("ropeq", i), ("eyeb",)], writes=[("sq", 2)])
                sb.op("act", lambda e, tq=tq: e.activation(out=qstage[:], in_=tq, func=AF.Identity),
                      reads=[("sq", 1)], writes=[("qstage",)])
                sb.op("act", lambda e, tk=tk: e.activation(out=kstage[:], in_=tk, func=AF.Identity),
                      reads=[("sq", 2)], writes=[("kstage",)])
                sb.op("sp", lambda e, c=c, i=i: e.dma_start(
                    out=scr["vspill"][c], in_=vstage[i][:]),
                    reads=[("vstage", i)], writes=[("vspill", c)], dma=True)
                sb.op("sp", lambda e, c=c: e.dma_start(
                    out=scr["qspill"][c].rearrange("d p q -> p d q"),
                    in_=qstage[:].rearrange("p (d q) -> p d q", d=8)),
                    reads=[("qstage",)], writes=[("qspill", c)], dma=True)
                sb.op("sp", lambda e, c=c: e.dma_start(
                    out=scr["kspill"][c].rearrange("d p q -> p d q"),
                    in_=kstage[:].rearrange("p (d q) -> p d q", d=8)),
                    reads=[("kstage",)], writes=[("kspill", c)], dma=True)
            return emit_tail

        pending[0] = mk_pending()
        if bg is not None:
            for _ in range(9):
                next(bg, None)
    pending[0]()

    s1.close()
    sb.barrier()

    a2 = ExitStack()
    ofm = [sbuf(f"ofm{which}_{p}", (128, 2048), BF16, stack=a2) for p in range(NDC)]
    vbuf = [sbuf(f"vbuf{which}_{i}", (128, NTC * 130), BF16, stack=a2) for i in range(2)]
    Wp = [sbuf(f"Wp{which}_{dc}", (128, D), BF16, stack=a2) for dc in range(NDC)]
    bpro = sbuf(f"bpro_{which}", (128, D), F32, stack=a2)
    qbuf = [sbuf(f"qbuf{which}_{i}", (128, 2048), BF16, stack=a2) for i in range(2)]
    kbuf = [sbuf(f"kbuf{which}_{i}", (128, 2048), BF16, stack=a2) for i in range(2)]
    expq = [sbuf(f"expq{which}_{i}", (128, 1024), BF16, stack=a2) for i in range(2)]
    oh2 = [sbuf(f"oh2_{which}{j}", (64, 512), BF16, stack=a2) for j in range(2)]
    recip = sbuf(f"recip_{which}", (65, 1024), F32, stack=a2)
    lnden = sbuf(f"lnden_{which}", (65, 1024), F32, stack=a2)
    otb = [sbuf(f"otb_{which}{j}", (65, 1024), F32, stack=a2) for j in range(2)]
    prtmp = [sbuf(f"prtmp{which}_{i}", (128, 512), F32, stack=a2) for i in range(2)]

    for dc in range(NDC):
        sb.op("gp", lambda e, dc=dc: e.dma_start(
            out=Wp[dc][:], in_=dr[pname][dc * 128:(dc + 1) * 128, :]),
            writes=[("Wp", dc)], dma=True)
    sb.op("sp", lambda e: e.dma_start(out=bpro[:], in_=dr[pbias][None, :].partition_broadcast(128)),
          writes=[("bpro",)], dma=True)

    def fetch_pair(p):
        i = p % 2
        sb.op("sp", lambda e, p=p, i=i: e.dma_start(
            out=qbuf[i][:].rearrange("p (t q) -> p t q", q=128),
            in_=scr["qspill"][:, p].rearrange("t p q -> p t q")),
            reads=[("qspill", c2) for c2 in range(NTC)], writes=[("qbuf", i)], dma=True)
        sb.op("sp", lambda e, p=p, i=i: e.dma_start(
            out=kbuf[i][:].rearrange("p (t q) -> p t q", q=128),
            in_=scr["kspill"][:, p].rearrange("t p q -> p t q")),
            reads=[("kspill", c2) for c2 in range(NTC)], writes=[("kbuf", i)], dma=True)
        sb.op("sp", lambda e, p=p, i=i: e.dma_start(
            out=vbuf[i][:].rearrange("p (t v) -> p t v", v=130),
            in_=scr["vspill"][:, :, 2 * p * 65:(2 * p + 2) * 65].rearrange("t p v -> p t v")),
            reads=[("vspill", c2) for c2 in range(NTC)], writes=[("vbuf", i)], dma=True)

    segs = []
    for p in range(NDC):
        for g in range(4):
            nch = 4 * g + 4 if causal else NTC
            segs.append((p, g, nch))
    units = []
    for si, (p, g, nch) in enumerate(segs):
        for cch in range(nch):
            units.append((si, cch))
    nunits = len(units)

    def emit_S(u):
        si, cch = units[u]
        p, g, nch = segs[si]
        i = p % 2
        qt0 = g * 512
        sqh = u % 2
        for hi in range(2):
            sb.op("pe", lambda e, i=i, cch=cch, hi=hi, sqh=sqh, qt0=qt0: e.matmul(
                squad[:, sqh * 1024 + hi * 512:sqh * 1024 + (hi + 1) * 512],
                kbuf[i][hi * 64:(hi + 1) * 64, cch * 128:(cch + 1) * 128],
                qbuf[i][hi * 64:(hi + 1) * 64, qt0:qt0 + 512],
                start=True, stop=True),
                reads=[("kbuf", i), ("qbuf", i)], writes=[("sq", 2 * sqh + hi)])

    def emit_exp(u):
        si, cch = units[u]
        p, g, nch = segs[si]
        qt0 = g * 512
        sqh = u % 2
        eb = u % 2
        sb.op("act", lambda e, sqh=sqh, eb=eb: e.activation(
            out=expq[eb][:, 0:1024],
            in_=squad[:, sqh * 1024:(sqh + 1) * 1024], func=AF.Exp, scale=C["k"][:, 2:3]),
            reads=[("sq", 2 * sqh), ("sq", 2 * sqh + 1)], writes=[("expq", eb)])
        if causal:
            lo = cch * 128 - qt0
            if 0 <= lo < 512:
                for hi in range(2):
                    if lo > 0:
                        sb.op("pool", lambda e, eb=eb, hi=hi, lo=lo: e.memset(
                            expq[eb][:, hi * 512:hi * 512 + lo], 0.0),
                            writes=[("expq", eb)])
                    sb.op("pool", lambda e, eb=eb, hi=hi, lo=lo: e.tensor_tensor(
                        out=expq[eb][:, hi * 512 + lo:hi * 512 + lo + 128],
                        in0=expq[eb][:, hi * 512 + lo:hi * 512 + lo + 128],
                        in1=C["tri"][:], op=ALU.mult),
                        reads=[("tri",)], writes=[("expq", eb)])

    def emit_AV(u):
        si, cch = units[u]
        p, g, nch = segs[si]
        i = p % 2
        ob = 2 * (si % 2)
        eb = u % 2
        first = (cch == 0)
        last = (cch == nch - 1)
        for hi in range(2):
            v0 = cch * 130 + hi * 65
            sb.op("pe", lambda e, i=i, hi=hi, v0=v0, eb=eb, ob=ob, first=first, last=last: e.matmul(
                pbank[ob + hi][0:65, :],
                vbuf[i][:, v0:v0 + 65],
                expq[eb][:, hi * 512:(hi + 1) * 512],
                start=first, stop=last),
                reads=[("expq", eb), ("vbuf", i)], writes=[("pb", ob + hi)])

    def emit_finalize(si):
        p, g, nch = segs[si]
        ob = 2 * (si % 2)
        par = si % 2
        sb.op("dve", lambda e, ob=ob, par=par: e.tensor_copy(
            out=otb[par][0:65, 0:512], in_=pbank[ob][0:65, :]),
            reads=[("pb", ob)], writes=[("otb", par)])
        sb.op("dve", lambda e, ob=ob, par=par: e.tensor_copy(
            out=otb[par][0:65, 512:1024], in_=pbank[ob + 1][0:65, :]),
            reads=[("pb", ob + 1)], writes=[("otb", par)])
        sb.op("act", lambda e, par=par: e.activation(
            out=lnden[64:65, 0:1024], in_=otb[par][64:65, 0:1024], func=AF.Ln),
            reads=[("otb", par)], writes=[("lnden",)])
        sb.op("act", lambda e: e.activation(
            out=recip[64:65, 0:1024], in_=lnden[64:65, 0:1024],
            func=AF.Exp, scale=C["k"][64:65, 3:4]),
            reads=[("lnden",), ("k",)], writes=[("recip",)])
        for hi in range(2):
            sb.op("pe", lambda e, hi=hi, ob=ob: e.matmul(
                pbank[ob + hi][0:64, :], C["ones_row"][64:65, 0:64],
                recip[64:65, hi * 512:(hi + 1) * 512],
                start=True, stop=True),
                reads=[("ones_row",), ("recip",)], writes=[("pb", ob + hi)])
        sb.op("dve", lambda e, p=p, g=g, ob=ob, par=par: e.tensor_tensor(
            out=ofm[p][0:64, g * 512:(g + 1) * 512], in0=otb[par][0:64, 0:512],
            in1=pbank[ob][0:64, :], op=ALU.mult),
            reads=[("otb", par), ("pb", ob)], writes=[("ofm", p)])
        sb.op("dve", lambda e, ob=ob, par=par: e.tensor_tensor(
            out=oh2[par][0:64, :], in0=otb[par][0:64, 512:1024], in1=pbank[ob + 1][0:64, :], op=ALU.mult),
            reads=[("otb", par), ("pb", ob + 1)], writes=[("oh2", par)])
        sb.op("sp", lambda e, p=p, g=g, par=par: e.dma_start(
            out=ofm[p][64:128, g * 512:(g + 1) * 512], in_=oh2[par][0:64, :]),
            reads=[("oh2", par)], writes=[("ofm", p)], dma=True)

    fetch_pair(0)
    fetch_pair(1)
    LOOK = 2
    for u in range(LOOK):
        if u < nunits:
            emit_S(u)
    if nunits > 0:
        emit_exp(0)
    for u in range(nunits):
        si, cch = units[u]
        p, g, nch = segs[si]
        if g == 0 and cch == 0 and 1 <= p < NDC - 1:
            fetch_pair(p + 1)
        if u + LOOK < nunits:
            emit_S(u + LOOK)
        if u + 1 < nunits:
            emit_exp(u + 1)
        emit_AV(u)
        if cch == nch - 1:
            emit_finalize(si)
        if bg is not None:
            next(bg, None)
    if bg is not None:
        for _ in bg:
            pass

    sb.label = f"pj{which}"
    for c in range(NTC):
        for g2 in range(2):
            pbi = (2 * c + g2) % 4
            for p in range(NDC):
                sb.op("pe", lambda e, c=c, g2=g2, p=p, pbi=pbi: e.matmul(
                    pbank[pbi][:], ofm[p][:, c * 128:(c + 1) * 128],
                    Wp[p][:, g2 * 512:(g2 + 1) * 512],
                    start=(p == 0), stop=(p == NDC - 1)),
                    reads=[("ofm", p), ("Wp", p)], writes=[("pb", pbi)])
            j = (2 * c + g2) % 2
            sb.op("dve", lambda e, c=c, g2=g2, j=j, pbi=pbi: e.tensor_tensor(
                out=prtmp[j][:], in0=pbank[pbi][:], in1=x_tm[c][:, g2 * 512:(g2 + 1) * 512],
                op=ALU.add),
                reads=[("pb", pbi), ("x", c)], writes=[("prtmp", j)])
            sb.op("pool", lambda e, c=c, g2=g2, j=j: e.tensor_tensor(
                out=x_tm[c][:, g2 * 512:(g2 + 1) * 512], in0=prtmp[j][:],
                in1=bpro[:, g2 * 512:(g2 + 1) * 512], op=ALU.add),
                reads=[("prtmp", j), ("bpro",)], writes=[("x", c)])
    a2.close()


def wquant_gen(ctx, stack):
    nc, sb, dr, scr, C = ctx["nc"], ctx["sb"], ctx["dr"], ctx["scr"], ctx["C"]
    squad, pbank, sbuf = ctx["squad"], ctx["pbank"], ctx["sbuf"]
    QW = 1024
    wtmp = [sbuf(f"wq_tmp{i}", (128, QW), F32, stack=stack) for i in range(3)]
    zi = sbuf("wq_zi", (128, QW), I32, stack=stack)
    wqbf = sbuf("wq_bf", (128, QW), BF16, stack=stack)
    w1u = 4 * NDC
    w2u = NJC
    wsum = sbuf("wq_sum", (128, w1u + w2u), F32, stack=stack)
    wrow = sbuf("wq_row", (1, 64), F32, stack=stack)

    def w1_slice(u):
        dc, q = u // 4, u % 4
        return "w1T", dc * 128, q * QW, QW, 0, "w1qT_d", (dc, q)

    def w2_slice(u):
        return "w2T", u * 128, 0, D, 2, "w2qT_d", (u, 0)

    units = [w1_slice(u) for u in range(w1u)] + [w2_slice(u) for u in range(w2u)]
    for ui, (name, row0, cin, w, col, dst, dsl) in enumerate(units):
        i = ui % 3
        _lbl = sb.label
        sb.label = "wq"
        sb.op("sp", lambda e, name=name, row0=row0, cin=cin, w=w, i=i: e.dma_start(
            out=wtmp[i][:, 0:w], in_=dr[name][row0:row0 + 128, cin:cin + w]),
            writes=[("wqtmp", i)], dma=True)
        sb.label = _lbl
        yield
        sb.label = "wq"
        sb.op("dve", lambda e, ui=ui, i=i, w=w: e.tensor_reduce(
            out=wsum[:, ui:ui + 1], in_=wtmp[i][:, 0:w], axis=AX.X, op=ALU.add,
            apply_absolute_value=True),
            reads=[("wqtmp", i)], writes=[("wsum",)])
        sb.label = _lbl
        yield
    sb.label = "wq"
    sb.op("pe", lambda e: e.matmul(
        squad[0:1, 0:w1u + w2u], C["ones_row"][:, 0:1], wsum[:],
        start=True, stop=True),
        reads=[("ones_row",), ("wsum",)], writes=[("sq", 0)])
    sb.op("dve", lambda e: e.tensor_reduce(
        out=wrow[0:1, 0:1], in_=squad[0:1, 0:w1u], axis=AX.X, op=ALU.add),
        reads=[("sq", 0)], writes=[("wrow",)])
    sb.op("dve", lambda e: e.tensor_reduce(
        out=wrow[0:1, 1:2], in_=squad[0:1, w1u:w1u + w2u], axis=AX.X, op=ALU.add),
        reads=[("sq", 0)], writes=[("wrow",)])
    sb.op("dve", lambda e: e.tensor_scalar(
        out=wrow[0:1, 2:3], in0=wrow[0:1, 0:1], scalar1=1.0 / (DH * D),
        scalar2=1e-5, op0=ALU.mult, op1=ALU.max),
        reads=[("wrow",)], writes=[("wrow",)])
    sb.op("dve", lambda e: e.tensor_scalar(
        out=wrow[0:1, 3:4], in0=wrow[0:1, 1:2], scalar1=1.0 / (DH * D),
        scalar2=1e-5, op0=ALU.mult, op1=ALU.max),
        reads=[("wrow",)], writes=[("wrow",)])
    sb.op("pe", lambda e: e.matmul(
        squad[0:128, 0:2], C["ones_row"][0:1, :], wrow[0:1, 2:4],
        start=True, stop=True),
        reads=[("ones_row",), ("wrow",)], writes=[("sq", 0)])
    sb.op("dve", lambda e: e.reciprocal(out=C["wsc"][:, 0:1], in_=squad[:, 0:1]),
          reads=[("sq", 0)], writes=[("wsc",)])
    sb.op("dve", lambda e: e.tensor_scalar(
        out=C["wsc"][:, 1:2], in0=squad[:, 0:1], scalar1=1.0 / 127.0, scalar2=None,
        op0=ALU.mult),
        reads=[("sq", 0)], writes=[("wsc",)])
    sb.op("dve", lambda e: e.reciprocal(out=C["wsc"][:, 2:3], in_=squad[:, 1:2]),
          reads=[("sq", 0)], writes=[("wsc",)])
    sb.op("dve", lambda e: e.tensor_scalar(
        out=C["wsc"][:, 3:4], in0=squad[:, 1:2], scalar1=1.0 / 127.0, scalar2=None,
        op0=ALU.mult),
        reads=[("sq", 0)], writes=[("wsc",)])
    yield
    for ui, (name, row0, cin, w, col, dst, dsl) in enumerate(units):
        i = ui % 3
        _lbl = sb.label
        sb.label = "wq"
        sb.op("sp", lambda e, name=name, row0=row0, cin=cin, w=w, i=i: e.dma_start(
            out=wtmp[i][:, 0:w], in_=dr[name][row0:row0 + 128, cin:cin + w]),
            writes=[("wqtmp", i)], dma=True)
        yield
        sb.op("dve", lambda e, i=i, col=col, w=w: e.tensor_scalar(
            out=wtmp[i][:, 0:w], in0=wtmp[i][:, 0:w],
            scalar1=C["wsc"][:, col:col + 1], scalar2=1.0, op0=ALU.mult, op1=ALU.min),
            reads=[("wqtmp", i), ("wsc",)], writes=[("wqtmp", i)])
        yield
        sb.op("dve", lambda e, i=i, w=w: e.tensor_scalar(
            out=zi[:, 0:w], in0=wtmp[i][:, 0:w], scalar1=-1.0, scalar2=None,
            op0=ALU.max),
            reads=[("wqtmp", i)], writes=[("wqzi",)])
        yield
        sb.op("dve", lambda e, w=w: e.tensor_copy(out=wqbf[:, 0:w], in_=zi[:, 0:w]),
              reads=[("wqzi",)], writes=[("wqbf",)])
        yield
        if dst == "w1qT_d":
            dc, q4 = dsl
            sb.op("sp", lambda e, dc=dc, q4=q4, w=w: e.dma_start(
                out=scr["w1qT_d"][q4 * 8:q4 * 8 + 8, :, dc * 128:(dc + 1) * 128].rearrange("j p q -> p j q"),
                in_=wqbf[:, 0:w].rearrange("p (j q) -> p j q", q=128)),
                reads=[("wqbf",)], writes=[("w1qT_d", q4)], dma=True)
        else:
            u = dsl[0]
            for og in range(2):
                sb.op("sp", lambda e, u=u, og=og: e.dma_start(
                    out=scr["w2qT_d"][og, u // 2, :, u % 2, :],
                    in_=wqbf[:, og * 512:og * 512 + 512]),
                    reads=[("wqbf",)], writes=[("w2qT_d", u)], dma=True)
        sb.label = _lbl
        yield
    ctx["wquant_done"] = True


def emit_ffn(ctx, first):
    nc, sb, dr, scr, C = ctx["nc"], ctx["sb"], ctx["dr"], ctx["scr"], ctx["C"]
    x_tm, squad, pbank, sbuf = ctx["x_tm"], ctx["squad"], ctx["pbank"], ctx["sbuf"]
    tag = "B" if first else "DD"

    ph = ExitStack()
    tapw = sbuf(f"tapw_{tag}", (128, NJC * 3), F32, stack=ph)
    tapb = sbuf(f"tapb_{tag}", (128, NJC), F32, stack=ph)
    sb.op("sp", lambda e: e.dma_start(out=tapw[:], in_=dr["convw_t"]), writes=[("tapw",)], dma=True)
    sb.op("sp", lambda e: e.dma_start(out=tapb[:], in_=dr["convb_t"]), writes=[("tapb",)], dma=True)

    b1 = ExitStack()
    xq_fm = sbuf(f"xqfm_{tag}", (128, NDC * 2048), BF16, stack=ph)
    hsilu = sbuf(f"hsilu_{tag}", (128, NJC * 512), F32, stack=ph)
    amx = sbuf(f"amx_{tag}", (128, NTC), F32, stack=ph)
    sclt = sbuf(f"sclt_{tag}", (128, NTC), F32, stack=ph)
    r1t = sbuf(f"r1t_{tag}", (128, NTC), F32, stack=ph)
    s2tm = sbuf(f"s2tm_{tag}", (128, NTC), F32, stack=ph)
    r2t = sbuf(f"r2t_{tag}", (128, NTC), F32, stack=ph)
    rrow = sbuf(f"rrow_{tag}", (1, T + 2), F32, stack=ph)
    ln2f = [sbuf(f"ln2f_{tag}{i}", (128, D), F32, stack=b1) for i in range(2)]
    xqi = [sbuf(f"xqi_{tag}{i}", (128, D), I32, stack=b1) for i in range(2)]
    xqb = [sbuf(f"xqb_{tag}{i}", (128, D), BF16, stack=b1) for i in range(2)]

    emit_layernorm(sb, C, x_tm)
    for c in reversed(range(NTC)):
        i = c % 2
        sb.op("act", lambda e, c=c, i=i: e.activation(
            out=ln2f[i][:], in_=x_tm[c][:], func=AF.Identity,
            scale=C["rstd"][:, c:c + 1], bias=C["nmr"][:, c:c + 1]),
            reads=[("x", c), ("rstd",), ("nmr",)], writes=[("ln2f", i)])
        sb.op("dve", lambda e, c=c, i=i: e.tensor_reduce(
            out=amx[:, c:c + 1], in_=ln2f[i][:], axis=AX.X, op=ALU.max,
            apply_absolute_value=True),
            reads=[("ln2f", i)], writes=[("amx",)])
        sb.op("dve", lambda e, c=c: e.tensor_scalar(
            out=amx[:, c:c + 1], in0=amx[:, c:c + 1], scalar1=1e-5, scalar2=None,
            op0=ALU.max),
            reads=[("amx",)], writes=[("amx",)])
        sb.op("dve", lambda e, c=c: e.reciprocal(out=sclt[:, c:c + 1], in_=amx[:, c:c + 1]),
              reads=[("amx",)], writes=[("sclt",)])
        sb.op("dve", lambda e, c=c: e.tensor_scalar(
            out=sclt[:, c:c + 1], in0=sclt[:, c:c + 1], scalar1=127.0, scalar2=None,
            op0=ALU.mult),
            reads=[("sclt",)], writes=[("sclt",)])
        sb.op("dve", lambda e, c=c: e.tensor_scalar(
            out=r1t[:, c:c + 1], in0=amx[:, c:c + 1], scalar1=C["wsc"][:, 1:2],
            scalar2=None, op0=ALU.mult),
            reads=[("amx",), ("wsc",)], writes=[("r1t",)])
        sb.op("dve", lambda e, c=c, i=i: e.tensor_scalar(
            out=xqi[i][:], in0=ln2f[i][:], scalar1=sclt[:, c:c + 1], scalar2=None,
            op0=ALU.mult),
            reads=[("ln2f", i), ("sclt",)], writes=[("xqi", i)])
        sb.op("dve", lambda e, i=i: e.tensor_copy(out=xqb[i][:], in_=xqi[i][:]),
              reads=[("xqi", i)], writes=[("xqb", i)])
        tps = squad[:, 0:512].bitcast(BF16)
        for dc in range(NDC):
            sb.op("pe", lambda e, i=i, dc=dc, tps=tps: e.transpose(
                out=tps[:, dc * 128:(dc + 1) * 128],
                in_=xqb[i][:, dc * 128:(dc + 1) * 128], identity=C["eyeb"][:]),
                reads=[("xqb", i), ("eyeb",)], writes=[("sq", 0)])
        xv = xq_fm[:].rearrange("p (d t) -> p d t", t=2048)
        sb.op("dve", lambda e, c=c, tps=tps, xv=xv: e.tensor_copy(
            out=xv[:, :, c * 128:(c + 1) * 128],
            in_=tps[:].rearrange("p (d q) -> p d q", q=128)),
            reads=[("sq", 0)], writes=[("xqfm",)])
    if FFN_CUT == "b1":
        b1.close()
        ph.close()
        return
    r1T = sbuf(f"r1T_{tag}", (NTC, 128), F32, stack=b1)
    sb.op("pe", lambda e: e.transpose(
        squad[0:NTC, 0:128], r1t[:], identity=C["eyef"][:]),
        reads=[("r1t",), ("eyef",)], writes=[("sq", 0)])
    sb.op("dve", lambda e: e.tensor_copy(out=r1T[:], in_=squad[0:NTC, 0:128]),
          reads=[("sq", 0)], writes=[("r1T",)])
    sb.op("dve", lambda e: e.memset(rrow[:, 0:1], 0.0), writes=[("rrow",)])
    sb.op("dve", lambda e: e.memset(rrow[:, T + 1:T + 2], 0.0), writes=[("rrow",)])
    sb.op("sp", lambda e: e.dma_start(
        out=rrow[0:1, 1:T + 1].rearrange("o (t q) -> o t q", q=128),
        in_=r1T[:]),
        reads=[("r1T",)], writes=[("rrow",)], dma=True)
    b1.close()
    sb.barrier()
    if FFN_CUT == "b1r":
        ph.close()
        return

    b2 = ExitStack()
    wbuf = [sbuf(f"wbuf_{tag}{i}", (128, NDC * 128), BF16, stack=b2) for i in range(3)]
    w2buf = [sbuf(f"w2buf_{tag}{i}", (128, 1024), BF16, stack=b2) for i in range(2)]
    rbc = sbuf(f"rbc_{tag}", (128, 512), F32, stack=b2)
    rh1 = sbuf(f"rh1_{tag}", (128, 1), F32, stack=b2)
    hsv = sbuf(f"hsv_{tag}", (128, 2 * NJC), F32, stack=b2)
    hdeq = [sbuf(f"hdeq_{tag}{i}", (128, 514), F32, stack=b2) for i in range(2)]
    cvo = [sbuf(f"cvo_{tag}{i}", (128, 512), F32, stack=b2) for i in range(2)]
    ct0 = [sbuf(f"ct0_{tag}{i}", (128, 512), F32, stack=b2) for i in range(2)]
    hqi = sbuf(f"hqi_{tag}", (128, 512), I32, stack=b2)
    amac = sbuf(f"amac_{tag}", (128, 512), F32, stack=b2)
    stg = [sbuf(f"stg_{tag}{i}", (128, 1), F32, stack=b2) for i in range(2)]
    s2T = sbuf(f"s2T_{tag}", (1, 512), F32, stack=b2)
    for j in range(2):
        sb.op("dve", lambda e, j=j: e.memset(stg[j][:], 0.0), writes=[("stg", j)])
        sb.op("pool", lambda e, j=j: e.memset(hdeq[j][:, 513:514], 0.0), writes=[("hdeq", j)])
    hqb_all = hsilu[:].bitcast(BF16)

    def wload(jc):
        sb.op("sp", lambda e, jc=jc: e.dma_start(
            out=wbuf[jc % 3][:], in_=scr["w1qT_d"][jc]),
            reads=[("w1qT_d", jc // 8)], writes=[("wbuf", jc % 3)], dma=True)

    def w2load(jc, og, q):
        b = (jc // 2) % 2
        sb.op(q, lambda e, jc=jc, og=og, b=b: e.dma_start(
            out=w2buf[b][:].rearrange("p (j q) -> p j q", q=512),
            in_=scr["w2qT_d"][og, jc // 2]),
            reads=[("w2qT_d", jc), ("w2qT_d", jc + 1)], writes=[("w2b", b)], dma=True)

    def og_mms(tv, jc, og, dst):
        b = (jc // 2) % 2
        for s in range(4):
            sb.op("pe", lambda e, jc=jc, s=s, b=b, og=og, dst=dst: e.matmul(
                dst[s][:], hqb_all[:, jc * 1024 + s * 128:jc * 1024 + (s + 1) * 128],
                w2buf[b][:, (jc % 2) * 512:(jc % 2) * 512 + 512],
                start=(jc == 0), stop=(jc == NJC - 1)),
                reads=[("hsilu", jc), ("w2b", b)], writes=[("pb", s)])

    def og_res(tv, og):
        for s in range(4):
            c2 = tv * 4 + s
            lo = og * 512
            sb.op("dve", lambda e, s=s, c2=c2, lo=lo: e.scalar_tensor_tensor(
                out=x_tm[c2][:, lo:lo + 512], in0=pbank[s][:],
                scalar=r2t[:, c2:c2 + 1], in1=x_tm[c2][:, lo:lo + 512],
                op0=ALU.mult, op1=ALU.add),
                reads=[("pb", s), ("r2t",), ("x", c2)], writes=[("x", c2)])

    def emit_tail(tv):
        base = tv * 4
        for s in range(4):
            sb.op("pe", lambda e, s=s: e.transpose(
                pbank[1][:, s * 128:(s + 1) * 128], amac[:, s * 128:(s + 1) * 128],
                identity=C["eyef"][:]),
                reads=[("amac",), ("eyef",)], writes=[("pb", 1)])
        for s in range(4):
            c2 = base + s
            sb.op("dve", lambda e, s=s, c2=c2: e.tensor_reduce(
                out=s2tm[:, c2:c2 + 1], in_=pbank[1][:, s * 128:(s + 1) * 128],
                axis=AX.X, op=ALU.max),
                reads=[("pb", 1)], writes=[("s2tm",)])
        sb.op("dve", lambda e, base=base: e.tensor_scalar(
            out=s2tm[:, base:base + 4], in0=s2tm[:, base:base + 4], scalar1=1e-5,
            scalar2=None, op0=ALU.max),
            reads=[("s2tm",)], writes=[("s2tm",)])
        sb.op("dve", lambda e, base=base: e.tensor_scalar(
            out=r2t[:, base:base + 4], in0=s2tm[:, base:base + 4], scalar1=C["wsc"][:, 3:4],
            scalar2=None, op0=ALU.mult),
            reads=[("s2tm",), ("wsc",)], writes=[("r2t",)])
        sb.op("dve", lambda e, base=base: e.reciprocal(
            out=s2tm[:, base:base + 4], in_=s2tm[:, base:base + 4]),
            reads=[("s2tm",)], writes=[("s2tm",)])
        sb.op("dve", lambda e, base=base: e.tensor_scalar(
            out=s2tm[:, base:base + 4], in0=s2tm[:, base:base + 4], scalar1=127.0,
            scalar2=None, op0=ALU.mult),
            reads=[("s2tm",)], writes=[("s2tm",)])
        for s in range(4):
            j = s % 2
            c2 = base + s
            sb.op("dve", lambda e, j=j, c2=c2: e.tensor_copy(
                out=stg[j][:, 0:1], in_=s2tm[:, c2:c2 + 1]),
                reads=[("s2tm",)], writes=[("stg", j)])
            sb.op("pe", lambda e, j=j, s=s: e.transpose(
                pbank[1][0:1, s * 128:(s + 1) * 128], stg[j][:, 0:1], identity=C["eyef"][:]),
                reads=[("stg", j), ("eyef",)], writes=[("pb", 1)])
        sb.op("dve", lambda e: e.tensor_copy(out=s2T[0:1, :], in_=pbank[1][0:1, 0:512]),
              reads=[("pb", 1)], writes=[("s2T",)])
        for s in range(4):
            sb.op("pe", lambda e, s=s: e.matmul(
                squad[:, s * 128:(s + 1) * 128], C["ones_row"][0:1, :],
                s2T[0:1, s * 128:(s + 1) * 128],
                start=(s == 0), stop=(s == 3), skip_group_check=True),
                reads=[("ones_row",), ("s2T",)], writes=[("sq", 0)])

    tiles = [3] if FFN_CUT in ("t3", "jc1") else [3, 2, 1, 0]
    prev = None
    for tv in tiles:
        sb.label = f"ffn_{tag}t{tv}"
        t0 = tv * 512
        tokbase = t0 - 1 if tv > 0 else 0
        dst0 = 1 if tv == 0 else 0
        sb.op("pe", lambda e, tokbase=tokbase: e.matmul(
            squad[:, 0:512], C["ones_row"][0:1, :],
            rrow[0:1, tokbase + 1:tokbase + 513],
            start=True, stop=True),
            reads=[("ones_row",), ("rrow",)], writes=[("sq", 0)])
        sb.op("dve", lambda e: e.tensor_copy(out=rbc[:], in_=squad[:, 0:512]),
              reads=[("sq", 0)], writes=[("rbc",)])
        if tv == 3:
            sb.op("pe", lambda e: e.matmul(
                squad[:, 0:1], C["ones_row"][0:1, :], rrow[0:1, 2048:2049],
                start=True, stop=True),
                reads=[("ones_row",), ("rrow",)], writes=[("sq", 0)])
            sb.op("dve", lambda e: e.tensor_copy(out=rh1[:], in_=squad[:, 0:1]),
                  reads=[("sq", 0)], writes=[("rh1",)])
        for pf in range(3):
            wload(pf)
        def front(jc):
            i = jc % 2
            bank = jc % 4
            if tv == 0:
                sb.op("pool", lambda e, i=i: e.memset(hdeq[i][:, 0:1], 0.0),
                      writes=[("hdeq", i)])
            sb.op("dve", lambda e, i=i, bank=bank, dst0=dst0: e.tensor_tensor(
                out=hdeq[i][:, dst0:dst0 + 512], in0=squad[:, bank * 512:bank * 512 + 512],
                in1=rbc[:], op=ALU.mult),
                reads=[("sq", bank), ("rbc",)], writes=[("hdeq", i)])
            if tv == 3:
                hb = jc % 4
                sb.op("dve", lambda e, i=i, hb=hb: e.tensor_tensor(
                    out=hdeq[i][:, 512:513], in0=pbank[hb][:, 0:1], in1=rh1[:],
                    op=ALU.mult),
                    reads=[("pb", hb), ("rh1",)], writes=[("hdeq", i)])
            elif tv > 0:
                sb.op("pool", lambda e, i=i, jc=jc: e.tensor_copy(
                    out=hdeq[i][:, 512:514], in_=hsv[:, 2 * jc:2 * jc + 2]),
                    reads=[("hsv", jc)], writes=[("hdeq", i)])
            else:
                sb.op("pool", lambda e, i=i, jc=jc: e.tensor_copy(
                    out=hdeq[i][:, 513:514], in_=hsv[:, 2 * jc + 1:2 * jc + 2]),
                    reads=[("hsv", jc)], writes=[("hdeq", i)])
            if tv > 0:
                sb.op("pool", lambda e, i=i, jc=jc: e.tensor_copy(
                    out=hsv[:, 2 * jc:2 * jc + 2], in_=hdeq[i][:, 0:2]),
                    reads=[("hdeq", i)], writes=[("hsv", jc)])
            if jc > 1:
                back_amac(jc - 2)
            sb.op("act", lambda e, i=i, jc=jc: e.activation(
                out=cvo[i][:], in_=hdeq[i][:, 2:514], func=AF.Identity,
                scale=tapw[:, 3 * jc + 2:3 * jc + 3], bias=tapb[:, jc:jc + 1]),
                reads=[("hdeq", i), ("tapw",), ("tapb",)], writes=[("cvo", i)])
            sb.op("act", lambda e, i=i, jc=jc: e.activation(
                out=ct0[i][:, 256:512], in_=hdeq[i][:, 256:512], func=AF.Identity,
                scale=tapw[:, 3 * jc:3 * jc + 1], bias=0.0),
                reads=[("hdeq", i), ("tapw",)], writes=[("ct0", i)])
            sb.op("dve", lambda e, i=i, jc=jc: e.tensor_scalar(
                out=ct0[i][:, 0:256], in0=hdeq[i][:, 0:256],
                scalar1=tapw[:, 3 * jc:3 * jc + 1], scalar2=None, op0=ALU.mult),
                reads=[("hdeq", i), ("tapw",)], writes=[("ct0", i)])
            sb.op("dve", lambda e, i=i, jc=jc: e.scalar_tensor_tensor(
                out=cvo[i][:], in0=hdeq[i][:, 1:513], scalar=tapw[:, 3 * jc + 1:3 * jc + 2],
                in1=cvo[i][:], op0=ALU.mult, op1=ALU.add),
                reads=[("hdeq", i), ("tapw",), ("cvo", i)], writes=[("cvo", i)])

        def back_pool(jc):
            i = jc % 2
            sb.op("pool", lambda e, i=i: e.tensor_tensor(
                out=cvo[i][:], in0=cvo[i][:], in1=ct0[i][:], op=ALU.add),
                reads=[("cvo", i), ("ct0", i)], writes=[("cvo", i)])

        def back_silu(jc):
            i = jc % 2
            sb.op("act", lambda e, i=i, jc=jc: e.activation(
                out=hsilu[:, jc * 512:(jc + 1) * 512], in_=cvo[i][:], func=AF.Silu),
                reads=[("cvo", i)], writes=[("hsilu", jc)])
            sb.op("act", lambda e, i=i, jc=jc: e.activation(
                out=ct0[i][:], in_=hsilu[:, jc * 512:(jc + 1) * 512], func=AF.Abs),
                reads=[("hsilu", jc), ("cvo", i), ("ct0", i)], writes=[("ct0", i)])

        def back_amac(jc):
            i = jc % 2
            if jc == 0:
                sb.op("dve", lambda e, i=i: e.tensor_copy(out=amac[:], in_=ct0[i][:]),
                      reads=[("ct0", i)], writes=[("amac",)])
            else:
                sb.op("dve", lambda e, i=i: e.tensor_tensor(
                    out=amac[:], in0=amac[:], in1=ct0[i][:], op=ALU.max),
                    reads=[("amac",), ("ct0", i)], writes=[("amac",)])

        for jc in range(NJC):
            sb.label = f"ffn_{tag}t{tv}_jc{jc}"
            bank = jc % 4
            if prev is not None:
                if jc == 0:
                    w2load(0, 1, "sp")
                if jc % 2 == 0 and jc + 2 < NJC:
                    w2load(jc + 2, 1, "sp")
                og_mms(prev, jc, 1, pbank)
            for dc in range(NDC):
                sb.op("pe", lambda e, jc=jc, dc=dc, bank=bank, tokbase=tokbase: e.matmul(
                    squad[:, bank * 512:bank * 512 + 512],
                    wbuf[jc % 3][:, dc * 128:(dc + 1) * 128],
                    xq_fm[:, dc * 2048 + tokbase:dc * 2048 + tokbase + 512],
                    start=(dc == 0), stop=(dc == NDC - 1)),
                    reads=[("wbuf", jc % 3), ("xqfm",)], writes=[("sq", bank)])
            if tv == 3:
                hb = jc % 4
                for dc in range(NDC):
                    sb.op("pe", lambda e, jc=jc, dc=dc, hb=hb: e.matmul(
                        pbank[hb][:, 0:1],
                        wbuf[jc % 3][:, dc * 128:(dc + 1) * 128],
                        xq_fm[:, dc * 2048 + 2047:dc * 2048 + 2048],
                        start=(dc == 0), stop=(dc == NDC - 1)),
                        reads=[("wbuf", jc % 3), ("xqfm",)], writes=[("pb", hb)])
            if jc + 3 < NJC:
                wload(jc + 3)
            if jc > 0:
                back_pool(jc - 1)
            front(jc)
            if jc > 0:
                back_silu(jc - 1)
            if FFN_CUT == "jc1":
                b2.close()
                ph.close()
                return
        back_pool(NJC - 1)
        back_silu(NJC - 1)
        back_amac(NJC - 2)
        back_amac(NJC - 1)
        if prev is not None:
            og_res(prev, 1)
        sb.label = f"ffn_{tag}t{tv}_tl"
        emit_tail(tv)
        sb.label = f"ffn_{tag}t{tv}_q"
        for jc in range(NJC):
            sb.op("dve", lambda e, jc=jc: e.tensor_tensor(
                out=hqi[:], in0=hsilu[:, jc * 512:(jc + 1) * 512], in1=squad[:, 0:512],
                op=ALU.mult),
                reads=[("hsilu", jc), ("sq", 0)], writes=[("hqi",)])
            sb.op("dve", lambda e, jc=jc: e.tensor_copy(
                out=hqb_all[:, jc * 1024:jc * 1024 + 512], in_=hqi[:]),
                reads=[("hqi",)], writes=[("hsilu", jc)])
            if jc == 0:
                w2load(0, 0, "sp")
            if jc % 2 == 0 and jc + 2 < NJC:
                w2load(jc + 2, 0, "sp")
            og_mms(tv, jc, 0, pbank)
        og_res(tv, 0)
        prev = tv
    if FFN_CUT in ("t3", "jc1"):
        b2.close()
        ph.close()
        return
    sb.label = f"ffn_{tag}fl"
    for jc in range(NJC):
        if jc == 0:
            w2load(0, 1, "sp")
        if jc % 2 == 0 and jc + 2 < NJC:
            w2load(jc + 2, 1, "sp")
        og_mms(0, jc, 1, pbank)
    og_res(0, 1)
    b2.close()
    ph.close()



def _host_constants():
    pos = np.arange(T, dtype=np.float64)
    i = np.arange(32, dtype=np.float64)
    inv = 10000.0 ** (-i / 32.0)
    theta = pos[:, None] * inv[None, :]
    cos = np.cos(theta).astype(np.float32)
    sin = np.sin(theta).astype(np.float32)
    cos_sb = np.ascontiguousarray(cos.reshape(NTC, 128, 32).transpose(1, 0, 2))
    sin_sb = np.ascontiguousarray(sin.reshape(NTC, 128, 32).transpose(1, 0, 2))
    import ml_dtypes
    cext = np.broadcast_to(cos_sb[:, :, None, :], (128, NTC, 2, 32))
    cext_sb = np.ascontiguousarray(cext.reshape(128, NTC * 64)).astype(ml_dtypes.bfloat16)
    sext = np.stack([-sin_sb, sin_sb], axis=2)
    sext_sb = np.ascontiguousarray(sext.reshape(128, NTC * 64)).astype(ml_dtypes.bfloat16)
    kt = np.arange(128)
    tri = (kt[:, None] <= kt[None, :]).astype(np.float32)
    eye_bf = np.eye(128, dtype=ml_dtypes.bfloat16)
    eye_f32 = np.eye(128, dtype=np.float32)
    return dict(cext_sb=cext_sb, sext_sb=sext_sb, tri=tri, eye_bf=eye_bf, eye_f32=eye_f32)


def kernel(L0_attn_a_x, L0_attn_a_W_attn, L0_attn_a_b_attn, L0_attn_a_W_proj,
           L0_attn_a_b_proj, L0_attn_b_qkv_w, L0_attn_b_qkv_b, L0_attn_b_out_w,
           L0_attn_b_out_b, L0_ffn_w1, L0_ffn_conv_w, L0_ffn_conv_b, L0_ffn_w2):
    f32 = lambda a: np.ascontiguousarray(np.asarray(a), dtype=np.float32)
    x = f32(L0_attn_a_x)
    shared = dict(
        W_attn=f32(L0_attn_a_W_attn),
        b_attn=f32(L0_attn_a_b_attn),
        W_proj=f32(L0_attn_a_W_proj),
        b_proj=f32(L0_attn_a_b_proj),
        qkv_wT=f32(np.asarray(L0_attn_b_qkv_w).T),
        qkv_b=f32(L0_attn_b_qkv_b),
        out_wT=f32(np.asarray(L0_attn_b_out_w).T),
        out_b=f32(L0_attn_b_out_b),
        w1T=f32(np.asarray(L0_ffn_w1).T),
        convw_t=f32(np.asarray(L0_ffn_conv_w).reshape(NJC, 128, 3).transpose(1, 0, 2).reshape(128, NJC * 3)),
        convb_t=f32(np.asarray(L0_ffn_conv_b).reshape(NJC, 128).T),
        w2T=f32(np.asarray(L0_ffn_w2).T),
        **_host_constants(),
    )
    nc = build_program()
    in_maps = [dict(shared, x=x[b]) for b in range(8)]
    res = run_bass_kernel_spmd(nc, in_maps, core_ids=list(range(8)))
    return np.stack([res.results[b]["out"] for b in range(8)], axis=0)


# version 25
# speedup vs baseline: 1.1112x; 1.0581x over previous
import numpy as np
from contextlib import ExitStack

import concourse.bass as bass
import concourse.mybir as mybir
from concourse.bass_utils import run_bass_kernel_spmd

F32 = mybir.dt.float32
BF16 = mybir.dt.bfloat16
I32 = mybir.dt.int32
AF = mybir.ActivationFunctionType
ALU = mybir.AluOpType
AX = mybir.AxisListType

T = 2048
D = 1024
H = 16
HD = 64
NTC = T // 128
NDC = D // 128
DH = 4096
NJC = DH // 128
EPS_LN = 1e-5

PHASES = ("A", "B", "C", "D")
FFN_CUT = None

ENGINES = ("pe", "dve", "act", "pool", "sp", "gp")
STREAM_OF = {"pe": "pe", "dve": "dve", "act": "act", "pool": "gpq", "gp": "gpq", "sp": "sp"}
NO_SELF_EDGE = {"pe", "sp", "gp"}


class SB:

    NLANE = 8

    def __init__(self, nc):
        self.nc = nc
        self.streams = {"pe": [], "dve": [], "act": [], "gpq": [], "sp": []}
        self.count = {e: 0 for e in ("pe", "dve", "act", "pool")}
        self.qcount = {"sp": 0, "gp": 0}
        self.last_w = {}
        self.readers = {}
        self.waited = {}
        self.sems = {}
        self.label = ""

    def _waited(self, eng):
        return self.waited.setdefault(eng, {})

    def op(self, eng, fn, reads=(), writes=(), dma=False):
        need = {}
        for k in reads:
            w = self.last_w.get(k)
            if w is not None:
                need[w[0]] = max(need.get(w[0], 0), w[1])
        for k in writes:
            w = self.last_w.get(k)
            if w is not None:
                need[w[0]] = max(need.get(w[0], 0), w[1])
            for f, c in self.readers.get(k, {}).items():
                need[f] = max(need.get(f, 0), c)
        if dma:
            q = eng
            slot = self.qcount[q] % self.NLANE
            cycle = self.qcount[q] // self.NLANE
            self.qcount[q] += 1
            pid = (q, slot)
            if cycle > 0:
                need[pid] = max(need.get(pid, 0), 16 * cycle)
            my_id, my_cnt = pid, 16 * (cycle + 1)
            wm = self._waited(q)
        else:
            self.count[eng] += 1
            my_id, my_cnt = eng, self.count[eng]
            wm = self._waited(eng)
        waits = []
        for f, c in sorted(need.items(), key=str):
            if f == eng and eng in NO_SELF_EDGE:
                continue
            if c > wm.get(f, 0):
                waits.append((f, c))
                wm[f] = c
        sems = self.sems
        step = 16 if dma else 1
        stream = STREAM_OF[eng]

        lbl = self.label

        def emit(engobj, waits=waits, fn=fn, my_id=my_id, step=step, lbl=lbl):
            for f, c in waits:
                engobj.wait_ge(sems[f], c)
            ins = fn(engobj)
            if lbl:
                ins.annotate(lbl)
            ins.then_inc(sems[my_id], step)

        self.streams[stream].append(emit)
        for k in writes:
            self.last_w[k] = (my_id, my_cnt)
            self.readers[k] = {}
        for k in reads:
            self.readers.setdefault(k, {})[my_id] = my_cnt
        return my_cnt

    def _all_producers(self):
        cur = {e: c for e, c in self.count.items() if c > 0}
        for q in ("sp", "gp"):
            n = self.qcount[q]
            for slot in range(self.NLANE):
                uses = (n - slot + self.NLANE - 1) // self.NLANE
                if uses > 0:
                    cur[(q, slot)] = 16 * uses
        return cur

    def barrier(self):
        cur = self._all_producers()
        sems = self.sems
        for e in ("pe", "dve", "act", "pool", "sp", "gp"):
            wm = self._waited(e)
            waits = []
            for f, c in cur.items():
                if f == e:
                    continue
                if c > wm.get(f, 0):
                    waits.append((f, c))
                    wm[f] = c
            if not waits:
                continue

            def emit(engobj, waits=waits):
                for f, c in waits:
                    engobj.wait_ge(sems[f], c)

            self.streams[STREAM_OF[e]].append(emit)

    def finish(self):
        cur = self._all_producers()
        sems = self.sems

        def emit(engobj):
            for f, c in cur.items():
                engobj.wait_ge(sems[f], c)

        self.streams["sp"].append(emit)


def build_program():
    nc = bass.Bass("TRN2", target_bir_lowering=False, debug=False)
    sb = SB(nc)

    dr = {}
    def din(name, shape, dt=F32):
        dr[name] = nc.dram_tensor(name, list(shape), dt, kind="ExternalInput").ap()

    din("x", (T, D))
    din("W_attn", (D, 3 * D))
    din("b_attn", (3 * D,))
    din("W_proj", (D, D))
    din("b_proj", (D,))
    din("qkv_wT", (D, 3 * D))
    din("qkv_b", (3 * D,))
    din("out_wT", (D, D))
    din("out_b", (D,))
    din("w1T", (D, DH))
    din("convw_t", (128, NJC * 3))
    din("convb_t", (128, NJC))
    din("w2T", (DH, D))
    din("cext_sb", (128, NTC * 64), BF16)
    din("sext_sb", (128, NTC * 64), BF16)
    din("tri", (128, 128))
    din("eye_bf", (128, 128), BF16)
    din("eye_f32", (128, 128))

    out = nc.dram_tensor("out", [T, D], F32, kind="ExternalOutput").ap()

    scr = {}
    def dscr(name, shape, dt):
        scr[name] = nc.dram_tensor(name, list(shape), dt, kind="Internal").ap()

    dscr("qspill", (NTC, NDC, 128, 128), BF16)
    dscr("kspill", (NTC, NDC, 128, 128), BF16)
    dscr("vspill", (NTC, 128, H * 65), BF16)
    dscr("w1qT_d", (NJC, 128, D), BF16)
    dscr("w2qT_d", (2, NJC // 2, 128, 2, 512), BF16)

    with ExitStack() as st:
        for e in ("pe", "dve", "act", "pool"):
            sb.sems[e] = st.enter_context(nc.semaphore(f"sem_{e}"))
        for q in ("sp", "gp"):
            for slot in range(SB.NLANE):
                sb.sems[(q, slot)] = st.enter_context(nc.semaphore(f"sem_{q}{slot}"))

        squad = st.enter_context(nc.psum_tensor("squad", [128, 2048], F32))
        pbank = [st.enter_context(nc.psum_tensor(f"pb{i}", [128, 512], F32)) for i in range(4)]

        x_tm = [st.enter_context(nc.sbuf_tensor(f"x{c}", [128, D], F32)) for c in range(NTC)]

        ctx = dict(nc=nc, sb=sb, dr=dr, scr=scr, out=out, x_tm=x_tm,
                   squad=squad, pbank=pbank, st=st)
        emit_all(ctx)
        sb.finish()

        block = st.enter_context(nc.Block())

        @block.sync
        def _(sync):
            for f in sb.streams["sp"]:
                f(sync)

        @block.gpsimd
        def _(g):
            for f in sb.streams["gpq"]:
                f(g)

        @block.vector
        def _(v):
            for f in sb.streams["dve"]:
                f(v)

        @block.scalar
        def _(s):
            for f in sb.streams["act"]:
                f(s)

        @block.tensor
        def _(t):
            for f in sb.streams["pe"]:
                f(t)

    return nc



def emit_all(ctx):
    nc, sb, dr, st = ctx["nc"], ctx["sb"], ctx["dr"], ctx["st"]
    x_tm, out = ctx["x_tm"], ctx["out"]

    def sbuf(name, shape, dt, stack=st):
        return stack.enter_context(nc.sbuf_tensor(name, list(shape), dt))
    ctx["sbuf"] = sbuf

    C = {}
    ctx["C"] = C
    C["tri"] = sbuf("tri_s", (128, 128), F32)
    C["eyeb"] = sbuf("eyeb_s", (128, 128), BF16)
    C["eyef"] = sbuf("eyef_s", (128, 128), F32)
    C["ones_row"] = sbuf("ones_row", (128, 128), F32)
    C["Cext"] = sbuf("Cext", (128, NTC * 64), BF16)
    C["Sext"] = sbuf("Sext", (128, NTC * 64), BF16)
    C["mv"] = sbuf("mv_all", (128, 2 * NTC), F32)
    C["stat6"] = sbuf("stat6", (128, 12), F32)
    C["rstd"] = sbuf("rstd_all", (128, NTC), F32)
    C["nmr"] = sbuf("nmr_all", (128, NTC), F32)
    C["lnv"] = sbuf("lnv_all", (128, NTC), F32)
    C["wsc"] = sbuf("wsc_all", (128, 4), F32)

    for nm, src in (("tri", "tri"), ("eyeb", "eye_bf"), ("eyef", "eye_f32"),
                    ("Cext", "cext_sb"), ("Sext", "sext_sb")):
        sb.op("sp", lambda e, nm=nm, src=src: e.dma_start(out=C[nm][:], in_=dr[src]),
              writes=[(nm,)], dma=True)
    sb.op("dve", lambda e: e.memset(C["ones_row"][:], 1.0), writes=[("ones_row",)])
    C["k"] = sbuf("kconst", (128, 8), F32)
    for col, val in ((0, EPS_LN), (1, -0.5), (2, 0.125), (3, -1.0), (4, 127.0), (5, 1e-12)):
        sb.op("dve", lambda e, col=col, val=val: e.memset(C["k"][:, col:col + 1], val),
              writes=[("k",)])

    for c in range(NTC):
        sb.op("sp", lambda e, c=c: e.dma_start(
            out=x_tm[c][:], in_=dr["x"][c * 128:(c + 1) * 128, :]),
            writes=[("x", c)], dma=True)

    if "A" in PHASES:
        wq_stack = ExitStack()
        if "B" in PHASES or "D" in PHASES:
            bg = wquant_gen(ctx, wq_stack)
            next(bg, None)
        else:
            bg = None
        emit_attention(ctx, which="a", bg=bg)
        wq_stack.close()
    if "B" in PHASES:
        sb.barrier()
        emit_ffn(ctx, first=True)
    if "C" in PHASES:
        sb.barrier()
        emit_attention(ctx, which="b")
    if "D" in PHASES:
        sb.barrier()
        emit_ffn(ctx, first=False)

    sb.barrier()
    for c in range(NTC):
        sb.op("sp", lambda e, c=c: e.dma_start(
            out=out[c * 128:(c + 1) * 128, :], in_=x_tm[c][:]),
            reads=[("x", c)], dma=True)


def emit_layernorm(sb, C, x_tm):
    for c in range(NTC):
        sb.op("dve", lambda e, c=c: e.bn_stats(out=C["stat6"][:, 0:6], in_=x_tm[c][:, 0:512]),
              reads=[("x", c)], writes=[("stat6",)])
        sb.op("dve", lambda e, c=c: e.bn_stats(out=C["stat6"][:, 6:12], in_=x_tm[c][:, 512:1024]),
              reads=[("x", c)], writes=[("stat6",)])
        sb.op("dve", lambda e, c=c: e.bn_aggr(out=C["mv"][:, 2 * c:2 * c + 2], in_=C["stat6"][:]),
              reads=[("stat6",)], writes=[("mv",)])
    mvv = C["mv"][:].rearrange("p (c two) -> p c two", two=2)
    sb.op("act", lambda e: e.activation(out=C["lnv"][:], in_=mvv[:, :, 1], func=AF.Ln, bias=C["k"][:, 0:1]),
          reads=[("mv",)], writes=[("lnv",)])
    sb.op("act", lambda e: e.activation(out=C["rstd"][:], in_=C["lnv"][:], func=AF.Exp, scale=C["k"][:, 1:2]),
          reads=[("lnv",), ("k",)], writes=[("rstd",)])
    sb.op("dve", lambda e: e.scalar_tensor_tensor(
        out=C["nmr"][:], in0=mvv[:, :, 0], scalar=-1.0, in1=C["rstd"][:],
        op0=ALU.mult, op1=ALU.mult),
        reads=[("mv",), ("rstd",)], writes=[("nmr",)])


def emit_attention(ctx, which, bg=None):
    nc, sb, dr, scr, C = ctx["nc"], ctx["sb"], ctx["dr"], ctx["scr"], ctx["C"]
    x_tm, squad, pbank, sbuf = ctx["x_tm"], ctx["squad"], ctx["pbank"], ctx["sbuf"]
    causal = (which == "a")
    wname = "W_attn" if causal else "qkv_wT"
    bname = "b_attn" if causal else "qkv_b"
    pname = "W_proj" if causal else "out_wT"
    pbias = "b_proj" if causal else "out_b"

    s1 = ExitStack()
    vstage = [sbuf(f"vstage{which}_{i}", (128, H * 65), BF16, stack=s1) for i in range(2)]
    Wqkv = [sbuf(f"Wqkv{which}_{dc}", (128, 3 * D), BF16, stack=s1) for dc in range(NDC)]
    battrow = sbuf(f"battrow_{which}", (1, 3 * D), BF16, stack=s1)
    onesbf = sbuf(f"onesbf_{which}", (1, 128), BF16, stack=s1)
    ln = [sbuf(f"ln{which}_{i}", (128, D), BF16, stack=s1) for i in range(2)]
    lnT = [sbuf(f"lnT{which}_{i}", (128, D), BF16, stack=s1) for i in range(2)]
    qkbf = [sbuf(f"qkbf{which}_{i}", (128, 2048), BF16, stack=s1) for i in range(2)]
    ropet = [sbuf(f"ropet{which}_{i}", (128, 2048), BF16, stack=s1) for i in range(2)]
    ropeq = [sbuf(f"ropeq{which}_{i}", (128, 2048), BF16, stack=s1) for i in range(2)]
    qstage = sbuf(f"qstage_{which}", (128, 1024), BF16, stack=s1)
    kstage = sbuf(f"kstage_{which}", (128, 1024), BF16, stack=s1)
    if not causal:
        s1tmp = sbuf(f"s1tmp_{which}", (128, 1024), F32, stack=s1)
        s1b = sbuf(f"s1b_{which}", (128, 128), F32, stack=s1)

    for dc in range(NDC):
        sb.op("gp", lambda e, dc=dc: e.dma_start(
            out=Wqkv[dc][:], in_=dr[wname][dc * 128:(dc + 1) * 128, :]),
            writes=[("Wqkv", dc)], dma=True)
    sb.op("gp", lambda e: e.dma_start(out=battrow[:], in_=dr[bname][None, :]),
          writes=[("battrow",)], dma=True)
    sb.op("dve", lambda e: e.memset(onesbf[:], 1.0), writes=[("onesbf",)])

    for i in range(2):
        sb.op("dve", lambda e, i=i: e.memset(
            vstage[i][:].rearrange("p (h c65) -> p h c65", c65=65)[:, :, 64:65], 1.0),
            writes=[("vstage", i)])

    emit_layernorm(sb, C, x_tm)

    pending = [None]

    for c in range(NTC):
        sb.label = f"s1{which}_c{c}"
        i = c % 2
        sb.op("act", lambda e, c=c, i=i: e.activation(
            out=ln[i][:], in_=x_tm[c][:], func=AF.Identity,
            scale=C["rstd"][:, c:c + 1], bias=C["nmr"][:, c:c + 1]),
            reads=[("x", c), ("rstd",), ("nmr",)], writes=[("ln", i)])
        tps = squad[:, 0:512].bitcast(BF16)
        for dc in range(NDC):
            sb.op("pe", lambda e, i=i, dc=dc, tps=tps: e.transpose(
                out=tps[:, dc * 128:(dc + 1) * 128],
                in_=ln[i][:, dc * 128:(dc + 1) * 128], identity=C["eyeb"][:]),
                reads=[("ln", i), ("eyeb",)], writes=[("sq", 0)])
        sb.op("dve", lambda e, i=i, tps=tps: e.tensor_copy(out=lnT[i][:], in_=tps),
              reads=[("sq", 0)], writes=[("lnT", i)])
        for g in range(6):
            pb = pbank[g % 2]
            for dc in range(NDC):
                sb.op("pe", lambda e, i=i, dc=dc, g=g, pb=pb: e.matmul(
                    pb[:], lnT[i][:, dc * 128:(dc + 1) * 128],
                    Wqkv[dc][:, g * 512:(g + 1) * 512],
                    start=(dc == 0), stop=False),
                    reads=[("lnT", i), ("Wqkv", dc)], writes=[("pb", g % 2)])
            sb.op("pe", lambda e, g=g, pb=pb: e.matmul(
                pb[:], onesbf[0:1, :], battrow[0:1, g * 512:(g + 1) * 512],
                start=False, stop=True),
                reads=[("onesbf",), ("battrow",)], writes=[("pb", g % 2)])
            if g < 4:
                if causal:
                    dst = qkbf[i][:, g * 512:(g + 1) * 512]
                    sb.op("act", lambda e, dst=dst, pb=pb: e.activation(
                        out=dst, in_=pb[:], func=AF.Identity),
                        reads=[("pb", g % 2)], writes=[("qkbf", i)])
                else:
                    dst = qkbf[i][:, g * 512:(g + 1) * 512].rearrange(
                        "p (h b a) -> p h a b", b=2, a=32)
                    src0 = pb[:].rearrange("p (h a b) -> p h a b", a=32, b=2)
                    sb.op("act", lambda e, dst=dst, src0=src0: e.activation(
                        out=dst, in_=src0, func=AF.Identity),
                        reads=[("pb", g % 2)], writes=[("qkbf", i)])
            else:
                h0 = (g - 4) * 8
                vv = vstage[i][:].rearrange("p (h c65) -> p h c65", c65=65)[:, h0:h0 + 8, 0:64]
                sb.op("act", lambda e, pb=pb, vv=vv: e.activation(
                    out=vv, in_=pb[:].rearrange("p (h q) -> p h q", q=64), func=AF.Identity),
                    reads=[("pb", g % 2)], writes=[("vstage", i)])
        if pending[0] is not None:
            pending[0]()
        qv = qkbf[i][:].rearrange("p (g2 h i2) -> p g2 h i2", h=2, i2=32)
        cs = C["Cext"][:, c * 64:(c + 1) * 64].rearrange("p (h i2) -> p h i2", h=2)[:, None, :, :].broadcast_to((128, 32, 2, 32))
        sx = C["Sext"][:, c * 64:(c + 1) * 64].rearrange("p (h i2) -> p h i2", h=2)[:, None, :, :].broadcast_to((128, 32, 2, 32))
        rt = ropet[i][:].rearrange("p (g2 h i2) -> p g2 h i2", h=2, i2=32)
        rq = ropeq[i][:].rearrange("p (g2 h i2) -> p g2 h i2", h=2, i2=32)
        sb.op("dve", lambda e, qv=qv, sx=sx, rt=rt, i=i: e.tensor_tensor(
            out=rt[:, :, 0, :], in0=qv[:, :, 1, :], in1=sx[:, :, 0, :], op=ALU.mult),
            reads=[("qkbf", i), ("Sext",)], writes=[("ropet", i)])
        sb.op("dve", lambda e, qv=qv, sx=sx, rt=rt, i=i: e.tensor_tensor(
            out=rt[:, :, 1, :], in0=qv[:, :, 0, :], in1=sx[:, :, 1, :], op=ALU.mult),
            reads=[("qkbf", i), ("Sext",)], writes=[("ropet", i)])
        sb.op("dve", lambda e, qv=qv, cs=cs, rq=rq, i=i: e.tensor_tensor(
            out=rq, in0=qv, in1=cs, op=ALU.mult),
            reads=[("qkbf", i), ("Cext",)], writes=[("ropeq", i)])
        sb.op("dve", lambda e, i=i: e.tensor_tensor(
            out=ropeq[i][:], in0=ropeq[i][:], in1=ropet[i][:], op=ALU.add),
            reads=[("ropet", i), ("ropeq", i)], writes=[("ropeq", i)])
        if not causal:
            rv = ropeq[i][:].rearrange("p (g2 q) -> p g2 q", q=64)
            for half in range(2):
                sb.op("pool", lambda e, rv=rv, half=half: e.tensor_tensor(
                    out=s1tmp[:].rearrange("p (g2 q) -> p g2 q", q=32),
                    in0=rv[:, :, half * 32:(half + 1) * 32],
                    in1=rv[:, :, half * 32:(half + 1) * 32], op=ALU.mult),
                    reads=[("ropeq", i)], writes=[("s1tmp",)])
                sb.op("dve", lambda e, half=half: e.tensor_reduce(
                    out=s1b[:, half * 32:(half + 1) * 32],
                    in_=s1tmp[:].rearrange("p (g2 q) -> p g2 q", q=32),
                    axis=AX.X, op=ALU.add),
                    reads=[("s1tmp",)], writes=[("s1b",)])
            sb.op("dve", lambda e: e.tensor_tensor(
                out=s1b[:, 64:96], in0=s1b[:, 0:32], in1=s1b[:, 32:64], op=ALU.add),
                reads=[("s1b",)], writes=[("s1b",)])
            sb.op("act", lambda e: e.activation(out=s1b[:, 96:128], in_=s1b[:, 64:96], func=AF.Ln),
                  reads=[("s1b",)], writes=[("s1b",)])
            sb.op("act", lambda e: e.activation(out=s1b[:, 64:96], in_=s1b[:, 96:128], func=AF.Exp, scale=C["k"][:, 1:2]),
                  reads=[("s1b",), ("k",)], writes=[("s1b",)])
            sb.op("dve", lambda e: e.tensor_scalar(
                out=s1b[:, 64:96], in0=s1b[:, 64:96], scalar1=1e12, scalar2=None, op0=ALU.min),
                reads=[("s1b",)], writes=[("s1b",)])
            sb.op("dve", lambda e, rv=rv: e.tensor_tensor(
                out=rv, in0=rv, in1=s1b[:, 64:96][:, :, None].broadcast_to((128, 32, 64)),
                op=ALU.mult),
                reads=[("ropeq", i), ("s1b",)], writes=[("ropeq", i)])

        def mk_pending(c=c, i=i):
            def emit_tail():
                tq = squad[:, 512:1024].bitcast(BF16)
                tk = squad[:, 1024:1536].bitcast(BF16)
                for dc in range(NDC):
                    sb.op("pe", lambda e, i=i, dc=dc, tq=tq: e.transpose(
                        out=tq[:, dc * 128:(dc + 1) * 128],
                        in_=ropeq[i][:, dc * 128:(dc + 1) * 128], identity=C["eyeb"][:]),
                        reads=[("ropeq", i), ("eyeb",)], writes=[("sq", 1)])
                    sb.op("pe", lambda e, i=i, dc=dc, tk=tk: e.transpose(
                        out=tk[:, dc * 128:(dc + 1) * 128],
                        in_=ropeq[i][:, 1024 + dc * 128:1024 + (dc + 1) * 128], identity=C["eyeb"][:]),
                        reads=[("ropeq", i), ("eyeb",)], writes=[("sq", 2)])
                sb.op("act", lambda e, tq=tq: e.activation(out=qstage[:], in_=tq, func=AF.Identity),
                      reads=[("sq", 1)], writes=[("qstage",)])
                sb.op("act", lambda e, tk=tk: e.activation(out=kstage[:], in_=tk, func=AF.Identity),
                      reads=[("sq", 2)], writes=[("kstage",)])
                sb.op("sp", lambda e, c=c, i=i: e.dma_start(
                    out=scr["vspill"][c], in_=vstage[i][:]),
                    reads=[("vstage", i)], writes=[("vspill", c)], dma=True)
                sb.op("sp", lambda e, c=c: e.dma_start(
                    out=scr["qspill"][c].rearrange("d p q -> p d q"),
                    in_=qstage[:].rearrange("p (d q) -> p d q", d=8)),
                    reads=[("qstage",)], writes=[("qspill", c)], dma=True)
                sb.op("sp", lambda e, c=c: e.dma_start(
                    out=scr["kspill"][c].rearrange("d p q -> p d q"),
                    in_=kstage[:].rearrange("p (d q) -> p d q", d=8)),
                    reads=[("kstage",)], writes=[("kspill", c)], dma=True)
            return emit_tail

        pending[0] = mk_pending()
        if bg is not None:
            for _ in range(9):
                next(bg, None)
    pending[0]()

    s1.close()
    sb.barrier()

    a2 = ExitStack()
    ofm = [sbuf(f"ofm{which}_{p}", (128, 2048), BF16, stack=a2) for p in range(NDC)]
    vbuf = [sbuf(f"vbuf{which}_{i}", (128, NTC * 130), BF16, stack=a2) for i in range(2)]
    Wp = [sbuf(f"Wp{which}_{dc}", (128, D), BF16, stack=a2) for dc in range(NDC)]
    bpro = sbuf(f"bpro_{which}", (128, D), F32, stack=a2)
    qbuf = [sbuf(f"qbuf{which}_{i}", (128, 2048), BF16, stack=a2) for i in range(2)]
    kbuf = [sbuf(f"kbuf{which}_{i}", (128, 2048), BF16, stack=a2) for i in range(2)]
    expq = [sbuf(f"expq{which}_{i}", (128, 1024), BF16, stack=a2) for i in range(2)]
    oh2 = [sbuf(f"oh2_{which}{j}", (64, 512), BF16, stack=a2) for j in range(2)]
    recip = sbuf(f"recip_{which}", (65, 1024), F32, stack=a2)
    lnden = sbuf(f"lnden_{which}", (65, 1024), F32, stack=a2)
    otb = [sbuf(f"otb_{which}{j}", (65, 1024), F32, stack=a2) for j in range(2)]
    prtmp = [sbuf(f"prtmp{which}_{i}", (128, 512), F32, stack=a2) for i in range(2)]

    for dc in range(NDC):
        sb.op("gp", lambda e, dc=dc: e.dma_start(
            out=Wp[dc][:], in_=dr[pname][dc * 128:(dc + 1) * 128, :]),
            writes=[("Wp", dc)], dma=True)
    sb.op("sp", lambda e: e.dma_start(out=bpro[:], in_=dr[pbias][None, :].partition_broadcast(128)),
          writes=[("bpro",)], dma=True)

    def fetch_pair(p):
        i = p % 2
        sb.op("sp", lambda e, p=p, i=i: e.dma_start(
            out=qbuf[i][:].rearrange("p (t q) -> p t q", q=128),
            in_=scr["qspill"][:, p].rearrange("t p q -> p t q")),
            reads=[("qspill", c2) for c2 in range(NTC)], writes=[("qbuf", i)], dma=True)
        sb.op("sp", lambda e, p=p, i=i: e.dma_start(
            out=kbuf[i][:].rearrange("p (t q) -> p t q", q=128),
            in_=scr["kspill"][:, p].rearrange("t p q -> p t q")),
            reads=[("kspill", c2) for c2 in range(NTC)], writes=[("kbuf", i)], dma=True)
        sb.op("sp", lambda e, p=p, i=i: e.dma_start(
            out=vbuf[i][:].rearrange("p (t v) -> p t v", v=130),
            in_=scr["vspill"][:, :, 2 * p * 65:(2 * p + 2) * 65].rearrange("t p v -> p t v")),
            reads=[("vspill", c2) for c2 in range(NTC)], writes=[("vbuf", i)], dma=True)

    segs = []
    for p in range(NDC):
        for g in range(4):
            nch = 4 * g + 4 if causal else NTC
            segs.append((p, g, nch))
    units = []
    for si, (p, g, nch) in enumerate(segs):
        for cch in range(nch):
            units.append((si, cch))
    nunits = len(units)

    def emit_S(u):
        si, cch = units[u]
        p, g, nch = segs[si]
        i = p % 2
        qt0 = g * 512
        sqh = u % 2
        for hi in range(2):
            sb.op("pe", lambda e, i=i, cch=cch, hi=hi, sqh=sqh, qt0=qt0: e.matmul(
                squad[:, sqh * 1024 + hi * 512:sqh * 1024 + (hi + 1) * 512],
                kbuf[i][hi * 64:(hi + 1) * 64, cch * 128:(cch + 1) * 128],
                qbuf[i][hi * 64:(hi + 1) * 64, qt0:qt0 + 512],
                start=True, stop=True),
                reads=[("kbuf", i), ("qbuf", i)], writes=[("sq", 2 * sqh + hi)])

    def emit_exp(u):
        si, cch = units[u]
        p, g, nch = segs[si]
        qt0 = g * 512
        sqh = u % 2
        eb = u % 2
        sb.op("act", lambda e, sqh=sqh, eb=eb: e.activation(
            out=expq[eb][:, 0:1024],
            in_=squad[:, sqh * 1024:(sqh + 1) * 1024], func=AF.Exp, scale=C["k"][:, 2:3]),
            reads=[("sq", 2 * sqh), ("sq", 2 * sqh + 1)], writes=[("expq", eb)])
        if causal:
            lo = cch * 128 - qt0
            if 0 <= lo < 512:
                for hi in range(2):
                    if lo > 0:
                        sb.op("pool", lambda e, eb=eb, hi=hi, lo=lo: e.memset(
                            expq[eb][:, hi * 512:hi * 512 + lo], 0.0),
                            writes=[("expq", eb)])
                    sb.op("pool", lambda e, eb=eb, hi=hi, lo=lo: e.tensor_tensor(
                        out=expq[eb][:, hi * 512 + lo:hi * 512 + lo + 128],
                        in0=expq[eb][:, hi * 512 + lo:hi * 512 + lo + 128],
                        in1=C["tri"][:], op=ALU.mult),
                        reads=[("tri",)], writes=[("expq", eb)])

    def emit_AV(u):
        si, cch = units[u]
        p, g, nch = segs[si]
        i = p % 2
        ob = 2 * (si % 2)
        eb = u % 2
        first = (cch == 0)
        last = (cch == nch - 1)
        for hi in range(2):
            v0 = cch * 130 + hi * 65
            sb.op("pe", lambda e, i=i, hi=hi, v0=v0, eb=eb, ob=ob, first=first, last=last: e.matmul(
                pbank[ob + hi][0:65, :],
                vbuf[i][:, v0:v0 + 65],
                expq[eb][:, hi * 512:(hi + 1) * 512],
                start=first, stop=last),
                reads=[("expq", eb), ("vbuf", i)], writes=[("pb", ob + hi)])

    def emit_finalize(si):
        p, g, nch = segs[si]
        ob = 2 * (si % 2)
        par = si % 2
        sb.op("dve", lambda e, ob=ob, par=par: e.tensor_copy(
            out=otb[par][0:65, 0:512], in_=pbank[ob][0:65, :]),
            reads=[("pb", ob)], writes=[("otb", par)])
        sb.op("dve", lambda e, ob=ob, par=par: e.tensor_copy(
            out=otb[par][0:65, 512:1024], in_=pbank[ob + 1][0:65, :]),
            reads=[("pb", ob + 1)], writes=[("otb", par)])
        sb.op("act", lambda e, par=par: e.activation(
            out=lnden[64:65, 0:1024], in_=otb[par][64:65, 0:1024], func=AF.Ln),
            reads=[("otb", par)], writes=[("lnden",)])
        sb.op("act", lambda e: e.activation(
            out=recip[64:65, 0:1024], in_=lnden[64:65, 0:1024],
            func=AF.Exp, scale=C["k"][64:65, 3:4]),
            reads=[("lnden",), ("k",)], writes=[("recip",)])
        for hi in range(2):
            sb.op("pe", lambda e, hi=hi, ob=ob: e.matmul(
                pbank[ob + hi][0:64, :], C["ones_row"][64:65, 0:64],
                recip[64:65, hi * 512:(hi + 1) * 512],
                start=True, stop=True),
                reads=[("ones_row",), ("recip",)], writes=[("pb", ob + hi)])
        sb.op("dve", lambda e, p=p, g=g, ob=ob, par=par: e.tensor_tensor(
            out=ofm[p][0:64, g * 512:(g + 1) * 512], in0=otb[par][0:64, 0:512],
            in1=pbank[ob][0:64, :], op=ALU.mult),
            reads=[("otb", par), ("pb", ob)], writes=[("ofm", p)])
        sb.op("dve", lambda e, ob=ob, par=par: e.tensor_tensor(
            out=oh2[par][0:64, :], in0=otb[par][0:64, 512:1024], in1=pbank[ob + 1][0:64, :], op=ALU.mult),
            reads=[("otb", par), ("pb", ob + 1)], writes=[("oh2", par)])
        sb.op("sp", lambda e, p=p, g=g, par=par: e.dma_start(
            out=ofm[p][64:128, g * 512:(g + 1) * 512], in_=oh2[par][0:64, :]),
            reads=[("oh2", par)], writes=[("ofm", p)], dma=True)

    fetch_pair(0)
    fetch_pair(1)
    LOOK = 2
    for u in range(LOOK):
        if u < nunits:
            emit_S(u)
    if nunits > 0:
        emit_exp(0)
    for u in range(nunits):
        si, cch = units[u]
        p, g, nch = segs[si]
        if g == 0 and cch == 0 and 1 <= p < NDC - 1:
            fetch_pair(p + 1)
        if u + LOOK < nunits:
            emit_S(u + LOOK)
        if u + 1 < nunits:
            emit_exp(u + 1)
        emit_AV(u)
        if cch == nch - 1:
            emit_finalize(si)
        if bg is not None:
            next(bg, None)
    if bg is not None:
        for _ in bg:
            pass

    sb.label = f"pj{which}"
    for c in range(NTC):
        for g2 in range(2):
            pbi = (2 * c + g2) % 4
            for p in range(NDC):
                sb.op("pe", lambda e, c=c, g2=g2, p=p, pbi=pbi: e.matmul(
                    pbank[pbi][:], ofm[p][:, c * 128:(c + 1) * 128],
                    Wp[p][:, g2 * 512:(g2 + 1) * 512],
                    start=(p == 0), stop=(p == NDC - 1)),
                    reads=[("ofm", p), ("Wp", p)], writes=[("pb", pbi)])
            j = (2 * c + g2) % 2
            sb.op("dve", lambda e, c=c, g2=g2, j=j, pbi=pbi: e.tensor_tensor(
                out=prtmp[j][:], in0=pbank[pbi][:], in1=x_tm[c][:, g2 * 512:(g2 + 1) * 512],
                op=ALU.add),
                reads=[("pb", pbi), ("x", c)], writes=[("prtmp", j)])
            sb.op("pool", lambda e, c=c, g2=g2, j=j: e.tensor_tensor(
                out=x_tm[c][:, g2 * 512:(g2 + 1) * 512], in0=prtmp[j][:],
                in1=bpro[:, g2 * 512:(g2 + 1) * 512], op=ALU.add),
                reads=[("prtmp", j), ("bpro",)], writes=[("x", c)])
    a2.close()


def wquant_gen(ctx, stack):
    nc, sb, dr, scr, C = ctx["nc"], ctx["sb"], ctx["dr"], ctx["scr"], ctx["C"]
    squad, pbank, sbuf = ctx["squad"], ctx["pbank"], ctx["sbuf"]
    QW = 1024
    wtmp = [sbuf(f"wq_tmp{i}", (128, QW), F32, stack=stack) for i in range(3)]
    zi = sbuf("wq_zi", (128, QW), I32, stack=stack)
    wqbf = sbuf("wq_bf", (128, QW), BF16, stack=stack)
    w1u = 4 * NDC
    w2u = NJC
    wsum = sbuf("wq_sum", (128, w1u + w2u), F32, stack=stack)
    wrow = sbuf("wq_row", (1, 64), F32, stack=stack)

    def w1_slice(u):
        dc, q = u // 4, u % 4
        return "w1T", dc * 128, q * QW, QW, 0, "w1qT_d", (dc, q)

    def w2_slice(u):
        return "w2T", u * 128, 0, D, 2, "w2qT_d", (u, 0)

    units = [w1_slice(u) for u in range(w1u)] + [w2_slice(u) for u in range(w2u)]
    for ui, (name, row0, cin, w, col, dst, dsl) in enumerate(units):
        i = ui % 3
        _lbl = sb.label
        sb.label = "wq"
        sb.op("sp", lambda e, name=name, row0=row0, cin=cin, w=w, i=i: e.dma_start(
            out=wtmp[i][:, 0:w], in_=dr[name][row0:row0 + 128, cin:cin + w]),
            writes=[("wqtmp", i)], dma=True)
        sb.label = _lbl
        yield
        sb.label = "wq"
        sb.op("dve", lambda e, ui=ui, i=i, w=w: e.tensor_reduce(
            out=wsum[:, ui:ui + 1], in_=wtmp[i][:, 0:w], axis=AX.X, op=ALU.add,
            apply_absolute_value=True),
            reads=[("wqtmp", i)], writes=[("wsum",)])
        sb.label = _lbl
        yield
    sb.label = "wq"
    sb.op("pe", lambda e: e.matmul(
        squad[0:1, 0:w1u + w2u], C["ones_row"][:, 0:1], wsum[:],
        start=True, stop=True),
        reads=[("ones_row",), ("wsum",)], writes=[("sq", 0)])
    sb.op("dve", lambda e: e.tensor_reduce(
        out=wrow[0:1, 0:1], in_=squad[0:1, 0:w1u], axis=AX.X, op=ALU.add),
        reads=[("sq", 0)], writes=[("wrow",)])
    sb.op("dve", lambda e: e.tensor_reduce(
        out=wrow[0:1, 1:2], in_=squad[0:1, w1u:w1u + w2u], axis=AX.X, op=ALU.add),
        reads=[("sq", 0)], writes=[("wrow",)])
    sb.op("dve", lambda e: e.tensor_scalar(
        out=wrow[0:1, 2:3], in0=wrow[0:1, 0:1], scalar1=1.0 / (DH * D),
        scalar2=1e-5, op0=ALU.mult, op1=ALU.max),
        reads=[("wrow",)], writes=[("wrow",)])
    sb.op("dve", lambda e: e.tensor_scalar(
        out=wrow[0:1, 3:4], in0=wrow[0:1, 1:2], scalar1=1.0 / (DH * D),
        scalar2=1e-5, op0=ALU.mult, op1=ALU.max),
        reads=[("wrow",)], writes=[("wrow",)])
    sb.op("pe", lambda e: e.matmul(
        squad[0:128, 0:2], C["ones_row"][0:1, :], wrow[0:1, 2:4],
        start=True, stop=True),
        reads=[("ones_row",), ("wrow",)], writes=[("sq", 0)])
    sb.op("dve", lambda e: e.reciprocal(out=C["wsc"][:, 0:1], in_=squad[:, 0:1]),
          reads=[("sq", 0)], writes=[("wsc",)])
    sb.op("dve", lambda e: e.tensor_scalar(
        out=C["wsc"][:, 1:2], in0=squad[:, 0:1], scalar1=1.0 / 127.0, scalar2=None,
        op0=ALU.mult),
        reads=[("sq", 0)], writes=[("wsc",)])
    sb.op("dve", lambda e: e.reciprocal(out=C["wsc"][:, 2:3], in_=squad[:, 1:2]),
          reads=[("sq", 0)], writes=[("wsc",)])
    sb.op("dve", lambda e: e.tensor_scalar(
        out=C["wsc"][:, 3:4], in0=squad[:, 1:2], scalar1=1.0 / 127.0, scalar2=None,
        op0=ALU.mult),
        reads=[("sq", 0)], writes=[("wsc",)])
    yield
    for ui, (name, row0, cin, w, col, dst, dsl) in enumerate(units):
        i = ui % 3
        _lbl = sb.label
        sb.label = "wq"
        sb.op("sp", lambda e, name=name, row0=row0, cin=cin, w=w, i=i: e.dma_start(
            out=wtmp[i][:, 0:w], in_=dr[name][row0:row0 + 128, cin:cin + w]),
            writes=[("wqtmp", i)], dma=True)
        yield
        sb.op("dve", lambda e, i=i, col=col, w=w: e.tensor_scalar(
            out=wtmp[i][:, 0:w], in0=wtmp[i][:, 0:w],
            scalar1=C["wsc"][:, col:col + 1], scalar2=1.0, op0=ALU.mult, op1=ALU.min),
            reads=[("wqtmp", i), ("wsc",)], writes=[("wqtmp", i)])
        yield
        sb.op("dve", lambda e, i=i, w=w: e.tensor_scalar(
            out=zi[:, 0:w], in0=wtmp[i][:, 0:w], scalar1=-1.0, scalar2=None,
            op0=ALU.max),
            reads=[("wqtmp", i)], writes=[("wqzi",)])
        yield
        sb.op("dve", lambda e, w=w: e.tensor_copy(out=wqbf[:, 0:w], in_=zi[:, 0:w]),
              reads=[("wqzi",)], writes=[("wqbf",)])
        yield
        if dst == "w1qT_d":
            dc, q4 = dsl
            sb.op("sp", lambda e, dc=dc, q4=q4, w=w: e.dma_start(
                out=scr["w1qT_d"][q4 * 8:q4 * 8 + 8, :, dc * 128:(dc + 1) * 128].rearrange("j p q -> p j q"),
                in_=wqbf[:, 0:w].rearrange("p (j q) -> p j q", q=128)),
                reads=[("wqbf",)], writes=[("w1qT_d", q4)], dma=True)
        else:
            u = dsl[0]
            for og in range(2):
                sb.op("sp", lambda e, u=u, og=og: e.dma_start(
                    out=scr["w2qT_d"][og, u // 2, :, u % 2, :],
                    in_=wqbf[:, og * 512:og * 512 + 512]),
                    reads=[("wqbf",)], writes=[("w2qT_d", u)], dma=True)
        sb.label = _lbl
        yield
    ctx["wquant_done"] = True


def emit_ffn(ctx, first):
    nc, sb, dr, scr, C = ctx["nc"], ctx["sb"], ctx["dr"], ctx["scr"], ctx["C"]
    x_tm, squad, pbank, sbuf = ctx["x_tm"], ctx["squad"], ctx["pbank"], ctx["sbuf"]
    tag = "B" if first else "DD"

    ph = ExitStack()
    tapw = sbuf(f"tapw_{tag}", (128, NJC * 3), F32, stack=ph)
    tapb = sbuf(f"tapb_{tag}", (128, NJC), F32, stack=ph)
    sb.op("sp", lambda e: e.dma_start(out=tapw[:], in_=dr["convw_t"]), writes=[("tapw",)], dma=True)
    sb.op("sp", lambda e: e.dma_start(out=tapb[:], in_=dr["convb_t"]), writes=[("tapb",)], dma=True)

    b1 = ExitStack()
    xq_fm = sbuf(f"xqfm_{tag}", (128, NDC * 2048), BF16, stack=ph)
    hsilu = sbuf(f"hsilu_{tag}", (128, NJC * 512), F32, stack=ph)
    amx = sbuf(f"amx_{tag}", (128, NTC), F32, stack=ph)
    sclt = sbuf(f"sclt_{tag}", (128, NTC), F32, stack=ph)
    r1t = sbuf(f"r1t_{tag}", (128, NTC), F32, stack=ph)
    s2tm = sbuf(f"s2tm_{tag}", (128, NTC), F32, stack=ph)
    r2t = sbuf(f"r2t_{tag}", (128, NTC), F32, stack=ph)
    rrow = sbuf(f"rrow_{tag}", (1, T + 2), F32, stack=ph)
    ln2f = [sbuf(f"ln2f_{tag}{i}", (128, D), F32, stack=b1) for i in range(2)]
    xqi = [sbuf(f"xqi_{tag}{i}", (128, D), I32, stack=b1) for i in range(2)]
    xqb = [sbuf(f"xqb_{tag}{i}", (128, D), BF16, stack=b1) for i in range(2)]

    emit_layernorm(sb, C, x_tm)
    for c in reversed(range(NTC)):
        i = c % 2
        sb.op("act", lambda e, c=c, i=i: e.activation(
            out=ln2f[i][:], in_=x_tm[c][:], func=AF.Identity,
            scale=C["rstd"][:, c:c + 1], bias=C["nmr"][:, c:c + 1]),
            reads=[("x", c), ("rstd",), ("nmr",)], writes=[("ln2f", i)])
        sb.op("dve", lambda e, c=c, i=i: e.tensor_reduce(
            out=amx[:, c:c + 1], in_=ln2f[i][:], axis=AX.X, op=ALU.max,
            apply_absolute_value=True),
            reads=[("ln2f", i)], writes=[("amx",)])
        sb.op("dve", lambda e, c=c: e.tensor_scalar(
            out=amx[:, c:c + 1], in0=amx[:, c:c + 1], scalar1=1e-5, scalar2=None,
            op0=ALU.max),
            reads=[("amx",)], writes=[("amx",)])
        sb.op("dve", lambda e, c=c: e.reciprocal(out=sclt[:, c:c + 1], in_=amx[:, c:c + 1]),
              reads=[("amx",)], writes=[("sclt",)])
        sb.op("dve", lambda e, c=c: e.tensor_scalar(
            out=sclt[:, c:c + 1], in0=sclt[:, c:c + 1], scalar1=127.0, scalar2=None,
            op0=ALU.mult),
            reads=[("sclt",)], writes=[("sclt",)])
        sb.op("dve", lambda e, c=c: e.tensor_scalar(
            out=r1t[:, c:c + 1], in0=amx[:, c:c + 1], scalar1=C["wsc"][:, 1:2],
            scalar2=None, op0=ALU.mult),
            reads=[("amx",), ("wsc",)], writes=[("r1t",)])
        sb.op("dve", lambda e, c=c, i=i: e.tensor_scalar(
            out=xqi[i][:], in0=ln2f[i][:], scalar1=sclt[:, c:c + 1], scalar2=None,
            op0=ALU.mult),
            reads=[("ln2f", i), ("sclt",)], writes=[("xqi", i)])
        sb.op("dve", lambda e, i=i: e.tensor_copy(out=xqb[i][:], in_=xqi[i][:]),
              reads=[("xqi", i)], writes=[("xqb", i)])
        tps = squad[:, 0:512].bitcast(BF16)
        for dc in range(NDC):
            sb.op("pe", lambda e, i=i, dc=dc, tps=tps: e.transpose(
                out=tps[:, dc * 128:(dc + 1) * 128],
                in_=xqb[i][:, dc * 128:(dc + 1) * 128], identity=C["eyeb"][:]),
                reads=[("xqb", i), ("eyeb",)], writes=[("sq", 0)])
        xv = xq_fm[:].rearrange("p (d t) -> p d t", t=2048)
        sb.op("dve", lambda e, c=c, tps=tps, xv=xv: e.tensor_copy(
            out=xv[:, :, c * 128:(c + 1) * 128],
            in_=tps[:].rearrange("p (d q) -> p d q", q=128)),
            reads=[("sq", 0)], writes=[("xqfm",)])
    if FFN_CUT == "b1":
        b1.close()
        ph.close()
        return
    r1T = sbuf(f"r1T_{tag}", (NTC, 128), F32, stack=b1)
    sb.op("pe", lambda e: e.transpose(
        squad[0:NTC, 0:128], r1t[:], identity=C["eyef"][:]),
        reads=[("r1t",), ("eyef",)], writes=[("sq", 0)])
    sb.op("dve", lambda e: e.tensor_copy(out=r1T[:], in_=squad[0:NTC, 0:128]),
          reads=[("sq", 0)], writes=[("r1T",)])
    sb.op("dve", lambda e: e.memset(rrow[:, 0:1], 0.0), writes=[("rrow",)])
    sb.op("dve", lambda e: e.memset(rrow[:, T + 1:T + 2], 0.0), writes=[("rrow",)])
    sb.op("sp", lambda e: e.dma_start(
        out=rrow[0:1, 1:T + 1].rearrange("o (t q) -> o t q", q=128),
        in_=r1T[:]),
        reads=[("r1T",)], writes=[("rrow",)], dma=True)
    b1.close()
    sb.barrier()
    if FFN_CUT == "b1r":
        ph.close()
        return

    b2 = ExitStack()
    wbuf = [sbuf(f"wbuf_{tag}{i}", (128, NDC * 128), BF16, stack=b2) for i in range(3)]
    w2buf = [sbuf(f"w2buf_{tag}{i}", (128, 1024), BF16, stack=b2) for i in range(2)]
    rbc = sbuf(f"rbc_{tag}", (128, 512), F32, stack=b2)
    rh1 = sbuf(f"rh1_{tag}", (128, 1), F32, stack=b2)
    hsv = sbuf(f"hsv_{tag}", (128, 2 * NJC), F32, stack=b2)
    hdeq = [sbuf(f"hdeq_{tag}{i}", (128, 514), F32, stack=b2) for i in range(2)]
    cvo = [sbuf(f"cvo_{tag}{i}", (128, 512), F32, stack=b2) for i in range(2)]
    ct0 = [sbuf(f"ct0_{tag}{i}", (128, 512), F32, stack=b2) for i in range(2)]
    hqi = sbuf(f"hqi_{tag}", (128, 512), I32, stack=b2)
    amac = sbuf(f"amac_{tag}", (128, 512), F32, stack=b2)
    stg = [sbuf(f"stg_{tag}{i}", (128, 1), F32, stack=b2) for i in range(2)]
    s2T = sbuf(f"s2T_{tag}", (1, 512), F32, stack=b2)
    for j in range(2):
        sb.op("dve", lambda e, j=j: e.memset(stg[j][:], 0.0), writes=[("stg", j)])
        sb.op("pool", lambda e, j=j: e.memset(hdeq[j][:, 513:514], 0.0), writes=[("hdeq", j)])
    hqb_all = hsilu[:].bitcast(BF16)

    def wload(jc):
        sb.op("sp", lambda e, jc=jc: e.dma_start(
            out=wbuf[jc % 3][:], in_=scr["w1qT_d"][jc]),
            reads=[("w1qT_d", jc // 8)], writes=[("wbuf", jc % 3)], dma=True)

    def w2load(jc, og, q):
        b = (jc // 2) % 2
        sb.op(q, lambda e, jc=jc, og=og, b=b: e.dma_start(
            out=w2buf[b][:].rearrange("p (j q) -> p j q", q=512),
            in_=scr["w2qT_d"][og, jc // 2]),
            reads=[("w2qT_d", jc), ("w2qT_d", jc + 1)], writes=[("w2b", b)], dma=True)

    def og_mms(tv, jc, og, dst):
        b = (jc // 2) % 2
        for s in range(4):
            sb.op("pe", lambda e, jc=jc, s=s, b=b, og=og, dst=dst: e.matmul(
                dst[s][:], hqb_all[:, jc * 1024 + s * 128:jc * 1024 + (s + 1) * 128],
                w2buf[b][:, (jc % 2) * 512:(jc % 2) * 512 + 512],
                start=(jc == 0), stop=(jc == NJC - 1)),
                reads=[("hsilu", jc), ("w2b", b)], writes=[("pb", s)])

    def og_res(tv, og):
        for s in range(4):
            c2 = tv * 4 + s
            lo = og * 512
            sb.op("dve", lambda e, s=s, c2=c2, lo=lo: e.scalar_tensor_tensor(
                out=x_tm[c2][:, lo:lo + 512], in0=pbank[s][:],
                scalar=r2t[:, c2:c2 + 1], in1=x_tm[c2][:, lo:lo + 512],
                op0=ALU.mult, op1=ALU.add),
                reads=[("pb", s), ("r2t",), ("x", c2)], writes=[("x", c2)])

    def emit_tail(tv):
        base = tv * 4
        for s in range(4):
            sb.op("pe", lambda e, s=s: e.transpose(
                pbank[1][:, s * 128:(s + 1) * 128], amac[:, s * 128:(s + 1) * 128],
                identity=C["eyef"][:]),
                reads=[("amac",), ("eyef",)], writes=[("pb", 1)])
        for s in range(4):
            c2 = base + s
            sb.op("dve", lambda e, s=s, c2=c2: e.tensor_reduce(
                out=s2tm[:, c2:c2 + 1], in_=pbank[1][:, s * 128:(s + 1) * 128],
                axis=AX.X, op=ALU.max),
                reads=[("pb", 1)], writes=[("s2tm",)])
        sb.op("dve", lambda e, base=base: e.tensor_scalar(
            out=s2tm[:, base:base + 4], in0=s2tm[:, base:base + 4], scalar1=1e-5,
            scalar2=None, op0=ALU.max),
            reads=[("s2tm",)], writes=[("s2tm",)])
        sb.op("dve", lambda e, base=base: e.tensor_scalar(
            out=r2t[:, base:base + 4], in0=s2tm[:, base:base + 4], scalar1=C["wsc"][:, 3:4],
            scalar2=None, op0=ALU.mult),
            reads=[("s2tm",), ("wsc",)], writes=[("r2t",)])
        sb.op("dve", lambda e, base=base: e.reciprocal(
            out=s2tm[:, base:base + 4], in_=s2tm[:, base:base + 4]),
            reads=[("s2tm",)], writes=[("s2tm",)])
        sb.op("dve", lambda e, base=base: e.tensor_scalar(
            out=s2tm[:, base:base + 4], in0=s2tm[:, base:base + 4], scalar1=127.0,
            scalar2=None, op0=ALU.mult),
            reads=[("s2tm",)], writes=[("s2tm",)])
        for s in range(4):
            j = s % 2
            c2 = base + s
            sb.op("dve", lambda e, j=j, c2=c2: e.tensor_copy(
                out=stg[j][:, 0:1], in_=s2tm[:, c2:c2 + 1]),
                reads=[("s2tm",)], writes=[("stg", j)])
            sb.op("pe", lambda e, j=j, s=s: e.transpose(
                pbank[1][0:1, s * 128:(s + 1) * 128], stg[j][:, 0:1], identity=C["eyef"][:]),
                reads=[("stg", j), ("eyef",)], writes=[("pb", 1)])
        sb.op("dve", lambda e: e.tensor_copy(out=s2T[0:1, :], in_=pbank[1][0:1, 0:512]),
              reads=[("pb", 1)], writes=[("s2T",)])
        for s in range(4):
            sb.op("pe", lambda e, s=s: e.matmul(
                squad[:, s * 128:(s + 1) * 128], C["ones_row"][0:1, :],
                s2T[0:1, s * 128:(s + 1) * 128],
                start=(s == 0), stop=(s == 3), skip_group_check=True),
                reads=[("ones_row",), ("s2T",)], writes=[("sq", 0)])

    tiles = [3] if FFN_CUT in ("t3", "jc1") else [3, 2, 1, 0]
    prev = None
    for tv in tiles:
        sb.label = f"ffn_{tag}t{tv}"
        t0 = tv * 512
        tokbase = t0 - 1 if tv > 0 else 0
        dst0 = 1 if tv == 0 else 0
        sb.op("pe", lambda e, tokbase=tokbase: e.matmul(
            squad[:, 0:512], C["ones_row"][0:1, :],
            rrow[0:1, tokbase + 1:tokbase + 513],
            start=True, stop=True),
            reads=[("ones_row",), ("rrow",)], writes=[("sq", 0)])
        sb.op("dve", lambda e: e.tensor_copy(out=rbc[:], in_=squad[:, 0:512]),
              reads=[("sq", 0)], writes=[("rbc",)])
        if tv == 3:
            sb.op("pe", lambda e: e.matmul(
                squad[:, 0:1], C["ones_row"][0:1, :], rrow[0:1, 2048:2049],
                start=True, stop=True),
                reads=[("ones_row",), ("rrow",)], writes=[("sq", 0)])
            sb.op("dve", lambda e: e.tensor_copy(out=rh1[:], in_=squad[:, 0:1]),
                  reads=[("sq", 0)], writes=[("rh1",)])
        for pf in range(3):
            wload(pf)
        def front(jc):
            i = jc % 2
            bank = jc % 4
            if tv == 0:
                sb.op("pool", lambda e, i=i: e.memset(hdeq[i][:, 0:1], 0.0),
                      writes=[("hdeq", i)])
            sb.op("dve", lambda e, i=i, bank=bank, dst0=dst0: e.tensor_tensor(
                out=hdeq[i][:, dst0:dst0 + 512], in0=squad[:, bank * 512:bank * 512 + 512],
                in1=rbc[:], op=ALU.mult),
                reads=[("sq", bank), ("rbc",)], writes=[("hdeq", i)])
            if tv == 3:
                hb = jc % 4
                sb.op("dve", lambda e, i=i, hb=hb: e.tensor_tensor(
                    out=hdeq[i][:, 512:513], in0=pbank[hb][:, 0:1], in1=rh1[:],
                    op=ALU.mult),
                    reads=[("pb", hb), ("rh1",)], writes=[("hdeq", i)])
            elif tv > 0:
                sb.op("pool", lambda e, i=i, jc=jc: e.tensor_copy(
                    out=hdeq[i][:, 512:514], in_=hsv[:, 2 * jc:2 * jc + 2]),
                    reads=[("hsv", jc)], writes=[("hdeq", i)])
            else:
                sb.op("pool", lambda e, i=i, jc=jc: e.tensor_copy(
                    out=hdeq[i][:, 513:514], in_=hsv[:, 2 * jc + 1:2 * jc + 2]),
                    reads=[("hsv", jc)], writes=[("hdeq", i)])
            if tv > 0:
                sb.op("pool", lambda e, i=i, jc=jc: e.tensor_copy(
                    out=hsv[:, 2 * jc:2 * jc + 2], in_=hdeq[i][:, 0:2]),
                    reads=[("hdeq", i)], writes=[("hsv", jc)])
            if jc > 1:
                back_amac(jc - 2)
            sb.op("act", lambda e, i=i, jc=jc: e.activation(
                out=cvo[i][:], in_=hdeq[i][:, 2:514], func=AF.Identity,
                scale=tapw[:, 3 * jc + 2:3 * jc + 3], bias=tapb[:, jc:jc + 1]),
                reads=[("hdeq", i), ("tapw",), ("tapb",)], writes=[("cvo", i)])
            sb.op("dve", lambda e, i=i, jc=jc: e.scalar_tensor_tensor(
                out=cvo[i][:], in0=hdeq[i][:, 1:513], scalar=tapw[:, 3 * jc + 1:3 * jc + 2],
                in1=cvo[i][:], op0=ALU.mult, op1=ALU.add),
                reads=[("hdeq", i), ("tapw",), ("cvo", i)], writes=[("cvo", i)])
            sb.op("dve", lambda e, i=i, jc=jc: e.scalar_tensor_tensor(
                out=cvo[i][:], in0=hdeq[i][:, 0:512], scalar=tapw[:, 3 * jc:3 * jc + 1],
                in1=cvo[i][:], op0=ALU.mult, op1=ALU.add),
                reads=[("hdeq", i), ("tapw",), ("cvo", i)], writes=[("cvo", i)])

        def back_silu(jc):
            i = jc % 2
            sb.op("act", lambda e, i=i, jc=jc: e.activation(
                out=hsilu[:, jc * 512:(jc + 1) * 512], in_=cvo[i][:], func=AF.Silu),
                reads=[("cvo", i)], writes=[("hsilu", jc)])
            sb.op("act", lambda e, i=i, jc=jc: e.activation(
                out=ct0[i][:], in_=hsilu[:, jc * 512:(jc + 1) * 512], func=AF.Abs),
                reads=[("hsilu", jc), ("cvo", i), ("ct0", i)], writes=[("ct0", i)])

        def back_amac(jc):
            i = jc % 2
            if jc == 0:
                sb.op("dve", lambda e, i=i: e.tensor_copy(out=amac[:], in_=ct0[i][:]),
                      reads=[("ct0", i)], writes=[("amac",)])
            else:
                sb.op("dve", lambda e, i=i: e.tensor_tensor(
                    out=amac[:], in0=amac[:], in1=ct0[i][:], op=ALU.max),
                    reads=[("amac",), ("ct0", i)], writes=[("amac",)])

        for jc in range(NJC):
            sb.label = f"ffn_{tag}t{tv}_jc{jc}"
            bank = jc % 4
            if prev is not None:
                if jc == 0:
                    w2load(0, 1, "sp")
                if jc % 2 == 0 and jc + 2 < NJC:
                    w2load(jc + 2, 1, "sp")
                og_mms(prev, jc, 1, pbank)
            for dc in range(NDC):
                sb.op("pe", lambda e, jc=jc, dc=dc, bank=bank, tokbase=tokbase: e.matmul(
                    squad[:, bank * 512:bank * 512 + 512],
                    wbuf[jc % 3][:, dc * 128:(dc + 1) * 128],
                    xq_fm[:, dc * 2048 + tokbase:dc * 2048 + tokbase + 512],
                    start=(dc == 0), stop=(dc == NDC - 1)),
                    reads=[("wbuf", jc % 3), ("xqfm",)], writes=[("sq", bank)])
            if tv == 3:
                hb = jc % 4
                for dc in range(NDC):
                    sb.op("pe", lambda e, jc=jc, dc=dc, hb=hb: e.matmul(
                        pbank[hb][:, 0:1],
                        wbuf[jc % 3][:, dc * 128:(dc + 1) * 128],
                        xq_fm[:, dc * 2048 + 2047:dc * 2048 + 2048],
                        start=(dc == 0), stop=(dc == NDC - 1)),
                        reads=[("wbuf", jc % 3), ("xqfm",)], writes=[("pb", hb)])
            if jc + 3 < NJC:
                wload(jc + 3)
            front(jc)
            if jc > 0:
                back_silu(jc - 1)
            if FFN_CUT == "jc1":
                b2.close()
                ph.close()
                return
        back_silu(NJC - 1)
        back_amac(NJC - 2)
        back_amac(NJC - 1)
        if prev is not None:
            og_res(prev, 1)
        sb.label = f"ffn_{tag}t{tv}_tl"
        emit_tail(tv)
        sb.label = f"ffn_{tag}t{tv}_q"
        for jc in range(NJC):
            sb.op("dve", lambda e, jc=jc: e.tensor_tensor(
                out=hqi[:], in0=hsilu[:, jc * 512:(jc + 1) * 512], in1=squad[:, 0:512],
                op=ALU.mult),
                reads=[("hsilu", jc), ("sq", 0)], writes=[("hqi",)])
            sb.op("dve", lambda e, jc=jc: e.tensor_copy(
                out=hqb_all[:, jc * 1024:jc * 1024 + 512], in_=hqi[:]),
                reads=[("hqi",)], writes=[("hsilu", jc)])
            if jc == 0:
                w2load(0, 0, "sp")
            if jc % 2 == 0 and jc + 2 < NJC:
                w2load(jc + 2, 0, "sp")
            og_mms(tv, jc, 0, pbank)
        og_res(tv, 0)
        prev = tv
    if FFN_CUT in ("t3", "jc1"):
        b2.close()
        ph.close()
        return
    sb.label = f"ffn_{tag}fl"
    for jc in range(NJC):
        if jc == 0:
            w2load(0, 1, "sp")
        if jc % 2 == 0 and jc + 2 < NJC:
            w2load(jc + 2, 1, "sp")
        og_mms(0, jc, 1, pbank)
    og_res(0, 1)
    b2.close()
    ph.close()



def _host_constants():
    pos = np.arange(T, dtype=np.float64)
    i = np.arange(32, dtype=np.float64)
    inv = 10000.0 ** (-i / 32.0)
    theta = pos[:, None] * inv[None, :]
    cos = np.cos(theta).astype(np.float32)
    sin = np.sin(theta).astype(np.float32)
    cos_sb = np.ascontiguousarray(cos.reshape(NTC, 128, 32).transpose(1, 0, 2))
    sin_sb = np.ascontiguousarray(sin.reshape(NTC, 128, 32).transpose(1, 0, 2))
    import ml_dtypes
    cext = np.broadcast_to(cos_sb[:, :, None, :], (128, NTC, 2, 32))
    cext_sb = np.ascontiguousarray(cext.reshape(128, NTC * 64)).astype(ml_dtypes.bfloat16)
    sext = np.stack([-sin_sb, sin_sb], axis=2)
    sext_sb = np.ascontiguousarray(sext.reshape(128, NTC * 64)).astype(ml_dtypes.bfloat16)
    kt = np.arange(128)
    tri = (kt[:, None] <= kt[None, :]).astype(np.float32)
    eye_bf = np.eye(128, dtype=ml_dtypes.bfloat16)
    eye_f32 = np.eye(128, dtype=np.float32)
    return dict(cext_sb=cext_sb, sext_sb=sext_sb, tri=tri, eye_bf=eye_bf, eye_f32=eye_f32)


def kernel(L0_attn_a_x, L0_attn_a_W_attn, L0_attn_a_b_attn, L0_attn_a_W_proj,
           L0_attn_a_b_proj, L0_attn_b_qkv_w, L0_attn_b_qkv_b, L0_attn_b_out_w,
           L0_attn_b_out_b, L0_ffn_w1, L0_ffn_conv_w, L0_ffn_conv_b, L0_ffn_w2):
    f32 = lambda a: np.ascontiguousarray(np.asarray(a), dtype=np.float32)
    x = f32(L0_attn_a_x)
    shared = dict(
        W_attn=f32(L0_attn_a_W_attn),
        b_attn=f32(L0_attn_a_b_attn),
        W_proj=f32(L0_attn_a_W_proj),
        b_proj=f32(L0_attn_a_b_proj),
        qkv_wT=f32(np.asarray(L0_attn_b_qkv_w).T),
        qkv_b=f32(L0_attn_b_qkv_b),
        out_wT=f32(np.asarray(L0_attn_b_out_w).T),
        out_b=f32(L0_attn_b_out_b),
        w1T=f32(np.asarray(L0_ffn_w1).T),
        convw_t=f32(np.asarray(L0_ffn_conv_w).reshape(NJC, 128, 3).transpose(1, 0, 2).reshape(128, NJC * 3)),
        convb_t=f32(np.asarray(L0_ffn_conv_b).reshape(NJC, 128).T),
        w2T=f32(np.asarray(L0_ffn_w2).T),
        **_host_constants(),
    )
    nc = build_program()
    in_maps = [dict(shared, x=x[b]) for b in range(8)]
    res = run_bass_kernel_spmd(nc, in_maps, core_ids=list(range(8)))
    return np.stack([res.results[b]["out"] for b in range(8)], axis=0)
